# version 11
# baseline (speedup 1.0000x reference)
from contextlib import ExitStack

import numpy as np

import concourse.bass as bass
import concourse.mybir as mybir
from concourse.bass_utils import run_bass_kernel_spmd

F32 = mybir.dt.float32
BF16 = mybir.dt.bfloat16
AF = mybir.ActivationFunctionType
ALU = mybir.AluOpType
AX = mybir.AxisListType

MAGIC = float(np.float32(1.5 * 2 ** 23))
NCORES = 8
S = 2048
E = 1024
FI = 2816
NT = S // 128
NG = 4
EC = E // 128
IC = FI // 128
EPS = 1e-5

ENGINES = ("pe", "act", "dve", "gp", "qs")


class Prog:

    def __init__(self):
        self.items = {e: [] for e in ENGINES}
        self.cum = {s: 0 for s in ("pe", "act", "dve", "gp_c")}
        self.state = {}
        self.waited = {e: {} for e in ENGINES}

    def _sem_for(self, stream, dma):
        if dma:
            return ("qd_" + dma, 16)
        if stream == "gp":
            return ("gp_c", 1)
        assert stream != "qs", "qs stream is DMA-only"
        return (stream, 1)

    def emit(self, stream, fn, reads=(), writes=(), dma=None):
        sem, inc = self._sem_for(stream, dma)
        if sem not in self.cum:
            self.cum[sem] = 0
        want = {}
        own = self.waited[stream]

        def need(dep):
            if dep is None:
                return
            dsem, dval = dep
            if dsem == "pe" and stream == "pe":
                return
            if dsem.startswith("qd_"):
                dval = self.cum[dsem]
            if own.get(dsem, 0) >= dval:
                return
            want[dsem] = max(want.get(dsem, 0), dval)

        for k in reads:
            st = self.state.get(k)
            if st:
                need(st["w"])
        for k in writes:
            st = self.state.get(k)
            if st:
                need(st["w"])
                for r in st["r"]:
                    need(r)
        waits = sorted(want.items())
        for dsem, dval in waits:
            own[dsem] = dval
        self.items[stream].append((waits, fn, sem, inc))
        self.cum[sem] += inc
        val = (sem, self.cum[sem])
        for k in reads:
            st = self.state.setdefault(k, {"w": None, "r": []})
            st["r"].append(val)
        for k in writes:
            self.state[k] = {"w": val, "r": []}
        return val

    def finish(self, block, sems):
        finals = dict(self.cum)

        def make_runner(stream, tail=False):
            def run(eng):
                for waits, fn, sem, inc in self.items[stream]:
                    for dsem, dval in waits:
                        eng.wait_ge(sems[dsem], dval)
                    fn(eng).then_inc(sems[sem], inc)
                if tail:
                    for dsem, fv in finals.items():
                        if dsem.startswith("qd_") and fv:
                            eng.wait_ge(sems[dsem], fv)
            return run

        block.sync(make_runner("qs", tail=True))
        if self.items["gp"]:
            block.gpsimd(make_runner("gp"))
        if self.items["pe"]:
            block.tensor(make_runner("pe"))
        if self.items["dve"]:
            block.vector(make_runner("dve"))
        if self.items["act"]:
            block.scalar(make_runner("act"))


class View:
    __slots__ = ("ap", "off")

    def __init__(self, ap, off):
        self.ap = ap
        self.off = off

    def __getitem__(self, k):
        return self.ap[k]


class Arena:
    def __init__(self, handle, width):
        self.h = handle
        self.w = width
        self.off = 0

    def alloc(self, parts, width):
        assert self.off + width <= self.w, f"arena overflow {self.off}+{width}"
        v = View(self.h[0:parts, self.off:self.off + width], self.off)
        self.off += width
        return v

    def alloc_n(self, n, parts, width):
        return [self.alloc(parts, width) for _ in range(n)]

    def seg_ap(self, view, col, seg_step, nseg, inner, parts=128):
        return bass.AP(self.h, view.off + col,
                       [[self.w, parts], [seg_step, nseg], [1, inner]])



def build_l1():
    nc = bass.Bass("TRN2", target_bir_lowering=False)
    p = Prog()
    em = p.emit

    x_h = nc.dram_tensor("x", [S, E], F32, kind="ExternalInput")
    wqkv_h = nc.dram_tensor("wqkv", [E, 3 * E], F32, kind="ExternalInput")
    qb_h = nc.dram_tensor("qb", [128, EC], F32, kind="ExternalInput")
    kb_h = nc.dram_tensor("kb", [128, EC], F32, kind="ExternalInput")
    vb_h = nc.dram_tensor("vb", [1, E], F32, kind="ExternalInput")
    wout_h = nc.dram_tensor("wout", [E, E], F32, kind="ExternalInput")
    bout_h = nc.dram_tensor("bout", [1, E], F32, kind="ExternalInput")
    cvec_h = nc.dram_tensor("cvec", [EC, 128], F32, kind="ExternalInput")
    wmod_h = nc.dram_tensor("wmod", [E, 6 * E], F32, kind="ExternalInput")
    wg_h = nc.dram_tensor("wg", [FI, E], F32, kind="ExternalInput")
    wu_h = nc.dram_tensor("wu", [FI, E], F32, kind="ExternalInput")
    wd_h = nc.dram_tensor("wd", [E, FI], F32, kind="ExternalInput")

    x2_h = nc.dram_tensor("x2", [S, E], F32, kind="ExternalOutput")
    hb_h = nc.dram_tensor("hb", [S, E], F32, kind="ExternalOutput")
    m1_h = nc.dram_tensor("m1loc", [1, 1], F32, kind="ExternalOutput")
    gbp_h = nc.dram_tensor("gbp", [1, E], F32, kind="ExternalOutput")
    wgq_h = nc.dram_tensor("wgqT", [E, FI], BF16, kind="ExternalOutput")
    wuq_h = nc.dram_tensor("wuqT", [E, FI], BF16, kind="ExternalOutput")
    wdq_h = nc.dram_tensor("wdqT", [FI, E], BF16, kind="ExternalOutput")
    sg_h = nc.dram_tensor("sg", [128, IC], F32, kind="ExternalOutput")
    su_h = nc.dram_tensor("su", [128, IC], F32, kind="ExternalOutput")
    sd_h = nc.dram_tensor("sd", [128, EC], F32, kind="ExternalOutput")

    with ExitStack() as ctx:
        ab_t = ctx.enter_context(nc.sbuf_tensor("ab", [128, 85400], BF16))
        af_t = ctx.enter_context(nc.sbuf_tensor("af", [128, 10050], F32))
        AB = Arena(ab_t, 85400)
        AFA = Arena(af_t, 10050)

        haT = AB.alloc_n(EC, 128, 512)
        wqkv = AB.alloc_n(EC, 128, 3072)
        KT = AB.alloc_n(EC, 128, S)
        V = AB.alloc_n(NT, 128, 16 * 65)
        QT = AB.alloc_n(2, 128, 512)
        wout = AB.alloc_n(EC, 128, E)
        attnT = AB.alloc_n(EC, 128, 512)
        PT = AB.alloc_n(2, 128, 1024)
        normed = AB.alloc(128, E)
        ovB = AB.alloc(128, 4864)
        wmod_s = [View(ovB[0:128, 512 * i:512 * (i + 1)], ovB.off + 512 * i)
                  for i in range(6)]
        wqi = [View(ovB[0:128, 0:1408], ovB.off),
               View(ovB[0:128, 1408:2816], ovB.off + 1408)]
        wqt = [View(ovB[0:128, 2816:3840], ovB.off + 2816),
               View(ovB[0:128, 3840:4864], ovB.off + 3840)]
        vb_row = AB.alloc(1, E)
        bout_row = AB.alloc(1, E)
        ones_row = AB.alloc(1, 128)
        idt_b = AB.alloc(128, 128)
        siluc = AB.alloc(128, EC)

        xs = AFA.alloc_n(2, 128, E)
        opcb_b, sb_b = xs[0], xs[1]
        ga_b = AFA.alloc(128, E)
        wnat = AFA.alloc_n(3, 128, 1024)
        ov1 = AFA.alloc(128, 3072)
        row_a = View(ov1[0:1, 0:1024], ov1.off)
        row_b = View(ov1[0:1, 1024:2048], ov1.off + 1024)
        row_c = View(ov1[0:1, 2048:3072], ov1.off + 2048)
        hbst = [View(ov1[0:128, 0:1024], ov1.off),
                View(ov1[0:128, 1024:2048], ov1.off + 1024)]
        recips = [View(ov1[0:64, 2048:2560], ov1.off + 2048),
                  View(ov1[0:64, 2560:3072], ov1.off + 2560)]
        onef_row = AFA.alloc(1, 128)
        mod_sa = AFA.alloc(128, EC)
        mod_opca = AFA.alloc(128, EC)
        mod_opcb = AFA.alloc(128, EC)
        mod_sb = AFA.alloc(128, EC)
        qb_cols = AFA.alloc(128, EC)
        kb_cols = AFA.alloc(128, EC)
        st_nm = AFA.alloc(128, NT)
        st_rs = AFA.alloc(128, NT)
        st_t1 = AFA.alloc(128, NT)
        st_t2 = AFA.alloc(128, NT)
        ssum = AFA.alloc(128, NT)
        ssq = AFA.alloc(128, NT)
        s2c = AFA.alloc(128, 6)
        idt_f = AFA.alloc(128, 128)
        iosc = AFA.alloc(128, 128)
        cband = View(iosc[0:EC, 0:128], iosc.off)
        m1run = AFA.alloc(128, 1)
        sc_t = AFA.alloc(128, 4)
        wsc_g = AFA.alloc(128, IC)
        wsc_u = AFA.alloc(128, IC)
        wsc_d = AFA.alloc(128, EC)
        winv_g = AFA.alloc(128, IC)
        winv_u = AFA.alloc(128, IC)
        winv_d = AFA.alloc(128, EC)
        eps_t = AFA.alloc(128, 1)
        magic_t = AFA.alloc(128, 1)
        nmagic_t = AFA.alloc(128, 1)

        psf = [ctx.enter_context(nc.psum_tensor(f"pf{i}", [128, 512], F32))
               for i in range(2)]
        psc = [ctx.enter_context(nc.psum_tensor(f"pc{i}", [128, 1024], F32))
               for i in range(2)]
        psb = [ctx.enter_context(nc.psum_tensor(f"pb{i}", [128, 1024], BF16))
               for i in range(2)]
        PSF = [f"ps{i}" for i in range(2)]
        PSC = [f"psc{i}" for i in range(2)]
        PSB = [f"psb{i}" for i in range(2)]

        ALIAS = {
            "hbst0": ["row_a"], "hbst1": ["row_b"],
            "recip0": ["row_c"], "recip1": ["row_c"],
            "wqi0": ["wmod0", "wmod1"], "wqi1": ["wmod2", "wmod3", "wmod4"],
            "wqt0": ["wmod5"], "wqt1": ["wmod5"],
            "cband": ["iosc"], "iosc": ["cband"],
        }
        _em0 = em

        def em2(stream, fn, reads=(), writes=(), dma=None):
            writes = list(writes)
            for w in list(writes):
                writes += ALIAS.get(w, [])
            return _em0(stream, fn, reads, writes, dma)
        em = em2

        em("gp", lambda g: g.iota(iosc[:], [[1, 128]], channel_multiplier=-1,
                                  allow_small_or_imprecise_dtypes=True),
           writes=["iosc"])
        em("dve", lambda v: v.tensor_single_scalar(idt_b[:], iosc[:], 0.0,
                                                   ALU.is_equal),
           reads=["iosc"], writes=["idt_b"])
        em("dve", lambda v: v.tensor_single_scalar(idt_f[:], iosc[:], 0.0,
                                                   ALU.is_equal),
           reads=["iosc"], writes=["idt_f"])
        em("dve", lambda v: v.memset(ones_row[:], 1.0), writes=["ones_row"])
        em("dve", lambda v: v.memset(onef_row[:], 1.0), writes=["onef_row"])
        em("dve", lambda v: v.memset(m1run[:], 0.0), writes=["m1run"])
        em("dve", lambda v: v.memset(eps_t[:], EPS), writes=["eps_t"])
        em("dve", lambda v: v.memset(magic_t[:], MAGIC), writes=["magic_t"])
        em("dve", lambda v: v.memset(nmagic_t[:], -MAGIC), writes=["nmagic_t"])

        em("qs", lambda s: s.dma_start(out=cband[:], in_=cvec_h[:, :]),
           writes=["cband"], dma="c")
        em("qs", lambda s: s.dma_start(out=qb_cols[:], in_=qb_h[:, :]),
           writes=["qb_cols"], dma="c")
        em("qs", lambda s: s.dma_start(out=kb_cols[:], in_=kb_h[:, :]),
           writes=["kb_cols"], dma="c")
        em("gp", lambda g: g.dma_start(out=vb_row[:], in_=vb_h[:, :]),
           writes=["vb_row"], dma="cg")
        em("gp", lambda g: g.dma_start(out=bout_row[:], in_=bout_h[:, :]),
           writes=["bout_row"], dma="cg")
        em("dve", lambda v: v.tensor_scalar_mul(qb_cols[:], qb_cols[:], 0.125),
           reads=["qb_cols"], writes=["qb_cols"])

        em("act", lambda a: a.activation(cband[:], cband[:], AF.Silu),
           reads=["cband"], writes=["cband"])
        em("pe", lambda t: t.transpose(psc[0][0:128, 0:EC], cband[:],
                                       idt_f[0:EC, 0:EC]),
           reads=["cband", "idt_f"], writes=["psc0"])
        em("dve", lambda v: v.tensor_copy(siluc[:], psc[0][0:128, 0:EC]),
           reads=["psc0"], writes=["siluc"])

        def row_to_cols(row_view, row_key, dst_view, dst_key):
            for c in range(EC):
                em("pe", lambda t, c=c: t.transpose(
                    psc[0][0:128, c:c + 1],
                    row_view[0:1, 128 * c:128 * (c + 1)],
                    idt_f[0:1, 0:1]),
                   reads=[row_key, "idt_f"], writes=["psc0"])
            em("dve", lambda v: v.tensor_copy(dst_view[:], psc[0][0:128, 0:EC]),
               reads=["psc0"], writes=[dst_key])

        def bcast_row(row_view, row_key, dst_view, dst_key):
            for hf in range(2):
                em("pe", lambda t, hf=hf: t.matmul(
                    psc[1][:, 0:512], onef_row[0:1, 0:128],
                    row_view[0:1, 512 * hf:512 * (hf + 1)],
                    start=True, stop=True),
                   reads=[row_key, "onef_row"], writes=["psc1"])
                em("dve", lambda v, hf=hf: v.tensor_copy(
                    dst_view[:, 512 * hf:512 * (hf + 1)], psc[1][:, 0:512]),
                   reads=["psc1"], writes=[dst_key])

        def sign_clamp(src_view, src_key):
            em("dve", lambda v: v.tensor_single_scalar(row_b[:], src_view[:],
                                                       0.0, ALU.is_ge),
               reads=[src_key], writes=["row_b"])
            em("dve", lambda v: v.tensor_scalar(row_b[:], row_b[:], 2.0, -1.0,
                                                ALU.mult, ALU.add),
               reads=["row_b"], writes=["row_b"])
            em("dve", lambda v: v.tensor_mul(row_c[:], src_view[:], row_b[:]),
               reads=[src_key, "row_b"], writes=["row_c"])
            em("dve", lambda v: v.tensor_scalar_max(row_c[:], row_c[:], 0.25),
               reads=["row_c"], writes=["row_c"])
            em("dve", lambda v: v.tensor_mul(row_b[:], row_b[:], row_c[:]),
               reads=["row_b", "row_c"], writes=["row_b"])

        for vv in range(6):
            for half in range(2):
                fg = 2 * vv + half
                for c in range(EC):
                    slot = (8 * fg + c) % 6
                    wk = f"wmod{slot}"
                    qq = "gp"
                    em(qq, lambda g, fg=fg, c=c, slot=slot: g.dma_start(
                        out=wmod_s[slot][:],
                        in_=wmod_h[128 * c:128 * (c + 1),
                                   512 * fg:512 * (fg + 1)]),
                       writes=[wk], dma=f"wm{slot}")
                    em("pe", lambda t, c=c, slot=slot: t.matmul(
                        psf[0][0:1, :], siluc[:, c:c + 1],
                        wmod_s[slot][:], start=(c == 0), stop=(c == EC - 1)),
                       reads=[wk, "siluc"], writes=["ps0"])
                em("dve", lambda v, half=half: v.tensor_copy(
                    row_a[0:1, 512 * half:512 * (half + 1)], psf[0][0:1, :]),
                   reads=["ps0"], writes=["row_a"])
            if vv == 0:
                row_to_cols(row_a, "row_a", mod_sa, "mod_sa")
            elif vv == 1:
                em("dve", lambda v: v.tensor_scalar_add(row_b[:], row_a[:], 1.0),
                   reads=["row_a"], writes=["row_b"])
                row_to_cols(row_b, "row_b", mod_opca, "mod_opca")
            elif vv == 2:
                sign_clamp(row_a, "row_a")
                bcast_row(row_b, "row_b", ga_b, "ga_b")
            elif vv == 3:
                row_to_cols(row_a, "row_a", mod_sb, "mod_sb")
            elif vv == 4:
                em("dve", lambda v: v.tensor_scalar_add(row_b[:], row_a[:], 1.0),
                   reads=["row_a"], writes=["row_b"])
                row_to_cols(row_b, "row_b", mod_opcb, "mod_opcb")
            else:
                sign_clamp(row_a, "row_a")
                em("qs", lambda s: s.dma_start(out=gbp_h[:, :], in_=row_b[:]),
                   reads=["row_b"], dma="sm")

        for tc in range(NT):
            xk = f"xs{tc % 2}"
            em("qs", lambda s, tc=tc: s.dma_start(
                out=xs[tc % 2][:], in_=x_h[128 * tc:128 * (tc + 1), :]),
               writes=[xk], dma=f"x{tc % 2}")
            em("act", lambda a, tc=tc: a.activation(
                normed[:], xs[tc % 2][:], AF.Square,
                accum_out=ssq[:, tc:tc + 1]),
               reads=[xk], writes=["normed", f"ssq{tc}"])
            em("dve", lambda v, tc=tc: v.reduce_sum(
                ssum[:, tc:tc + 1], xs[tc % 2][:], AX.X),
               reads=[xk], writes=[f"ssum{tc}"])
        sskeys = [f"ssq{t}" for t in range(NT)] + [f"ssum{t}" for t in range(NT)]
        em("dve", lambda v: v.tensor_scalar_mul(st_nm[:], ssum[:], -1.0 / E),
           reads=sskeys, writes=["st_nm"])
        em("dve", lambda v: v.tensor_scalar_mul(st_t1[:], ssq[:], 1.0 / E),
           reads=sskeys, writes=["st_t1"])
        em("dve", lambda v: v.tensor_mul(st_t2[:], st_nm[:], st_nm[:]),
           reads=["st_nm"], writes=["st_t2"])
        em("dve", lambda v: v.tensor_sub(st_t1[:], st_t1[:], st_t2[:]),
           reads=["st_t1", "st_t2"], writes=["st_t1"])
        em("act", lambda a: a.activation(st_t2[:], st_t1[:], AF.Sqrt,
                                         bias=eps_t[:, 0:1]),
           reads=["st_t1", "eps_t"], writes=["st_t2"])
        em("dve", lambda v: v.reciprocal(st_rs[:], st_t2[:]),
           reads=["st_t2"], writes=["st_rs"])

        def load_x(tc):
            xk = f"xs{tc % 2}"
            em("qs", lambda s, tc=tc: s.dma_start(
                out=xs[tc % 2][:], in_=x_h[128 * tc:128 * (tc + 1), :]),
               writes=[xk], dma=f"x{tc % 2}")
            return xk

        def make_haT(tc, tc4):
            xk = f"xs{tc % 2}"
            nk = "normed"
            pbk = f"psb{tc4 % 2}"
            em("dve", lambda v, tc=tc: v.tensor_scalar(
                normed[:], xs[tc % 2][:],
                st_nm[:, tc:tc + 1], st_rs[:, tc:tc + 1],
                ALU.add, ALU.mult),
               reads=[xk, "st_nm", "st_rs"], writes=[nk])
            for c in range(EC):
                em("pe", lambda t, tc=tc, tc4=tc4, c=c: t.transpose(
                    psb[tc4 % 2][:, 128 * c:128 * (c + 1)],
                    normed[:][:, 128 * c:128 * (c + 1)], idt_b[:]),
                   reads=[nk, "idt_b"], writes=[pbk])
            for c in range(EC):
                em("dve", lambda v, tc4=tc4, c=c: v.tensor_scalar(
                    haT[c][:, 128 * tc4:128 * (tc4 + 1)],
                    psb[tc4 % 2][:, 128 * c:128 * (c + 1)],
                    mod_opca[:, c:c + 1], mod_sa[:, c:c + 1],
                    ALU.mult, ALU.add),
                   reads=[pbk, "mod_opca", "mod_sa"],
                   writes=[f"haT{c}"])

        haT_keys = [f"haT{c}" for c in range(EC)]

        def seg_widths(rowlen):
            w, off = [], 0
            while off < rowlen:
                w.append(min(1024, rowlen - off))
                off += 1024
            return w

        wq_tasks = ([(wg_h, wgq_h, wsc_g, winv_g, "wsg", i, E)
                     for i in range(IC)]
                    + [(wu_h, wuq_h, wsc_u, winv_u, "wsu", i, E)
                       for i in range(IC)]
                    + [(wd_h, wdq_h, wsc_d, winv_d, "wsd", i, FI)
                       for i in range(EC)])
        _seg_seq = [0]

        def scale_task(task):
            for _t in [task]:
                (w_h, out_h, sc_view, winv_m, sc_key, i, rowlen) = _t
                for hseg, width in enumerate(seg_widths(rowlen)):
                    b = _seg_seq[0] % 3
                    _seg_seq[0] += 1
                    wk = f"wnat{b}"
                    em("gp", lambda g, w_h=w_h, i=i, width=width, b=b,
                       off=1024 * hseg: g.dma_start(
                           out=wnat[b][:][:, 0:width],
                           in_=w_h[128 * i:128 * (i + 1), off:off + width]),
                       writes=[wk], dma=f"wn{b}")
                    em("dve", lambda v, b=b, width=width: v.tensor_reduce(
                        sc_t[:, 2:3], wnat[b][:][:, 0:width], AX.X, ALU.max,
                        apply_absolute_value=True),
                       reads=[wk], writes=["wred"])
                    if hseg == 0:
                        em("dve", lambda v, sc_view=sc_view, i=i: v.tensor_copy(
                            sc_view[:, i:i + 1], sc_t[:, 2:3]),
                           reads=["wred"], writes=[f"{sc_key}{i}"])
                    else:
                        em("dve", lambda v, sc_view=sc_view, i=i: v.tensor_max(
                            sc_view[:, i:i + 1], sc_view[:, i:i + 1],
                            sc_t[:, 2:3]),
                           reads=["wred", f"{sc_key}{i}"],
                           writes=[f"{sc_key}{i}"])
        def scale_finish():
            for sc_view, winv_m, sc_key, n in ((wsc_g, winv_g, "wsg", IC),
                                               (wsc_u, winv_u, "wsu", IC),
                                               (wsc_d, winv_d, "wsd", EC)):
                sks = [f"{sc_key}{i}" for i in range(n)]
                em("dve", lambda v, winv_m=winv_m, sc_view=sc_view:
                   v.reciprocal(winv_m[:], sc_view[:]),
                   reads=sks, writes=[f"inv_{sc_key}"])
                em("dve", lambda v, winv_m=winv_m: v.tensor_scalar_mul(
                    winv_m[:], winv_m[:], 127.0),
                   reads=[f"inv_{sc_key}"], writes=[f"inv_{sc_key}"])
                em("dve", lambda v, sc_view=sc_view: v.tensor_scalar_mul(
                    sc_view[:], sc_view[:], 1.0 / 127.0),
                   reads=sks, writes=[sc_key])
            em("qs", lambda s: s.dma_start(out=sg_h[:, :], in_=wsc_g[:]),
               reads=["wsg"], dma="sm")
            em("qs", lambda s: s.dma_start(out=su_h[:, :], in_=wsc_u[:]),
               reads=["wsu"], dma="sm")
            em("qs", lambda s: s.dma_start(out=sd_h[:, :], in_=wsc_d[:]),
               reads=["wsd"], dma="sm")

        wq_rtasks = []
        for (w_h, out_h, sc_view, winv_m, sc_key, i, rowlen) in wq_tasks:
            for hseg, width in enumerate(seg_widths(rowlen)):
                wq_rtasks.append((w_h, out_h, winv_m, sc_key, i,
                                  1024 * hseg, width))

        def round_load(t):
            (w_h, out_h, winv_m, sc_key, i, off, width) = wq_rtasks[t]
            b = t % 3
            em("gp", lambda g, w_h=w_h, i=i, width=width, b=b, off=off:
               g.dma_start(
                   out=wnat[b][:][:, 0:width],
                   in_=w_h[128 * i:128 * (i + 1), off:off + width]),
               writes=[f"wnat{b}"], dma=f"wn{b}")
            return b

        def round_compute(t, b):
            (w_h, out_h, winv_m, sc_key, i, off, width) = wq_rtasks[t]
            wk = f"wnat{b}"
            qk = f"wqi{t % 2}"
            em("act", lambda a, b=b, width=width, winv_m=winv_m, i=i:
               a.activation(
                   wnat[b][:][:, 0:width], wnat[b][:][:, 0:width],
                   AF.Identity, scale=winv_m[:, i:i + 1],
                   bias=magic_t[:, 0:1]),
               reads=[wk, f"inv_{sc_key}", "magic_t"], writes=[wk])
            em("act", lambda a, b=b, width=width, t=t: a.activation(
                wqi[t % 2][:][:, 0:width], wnat[b][:][:, 0:width],
                AF.Identity, bias=nmagic_t[:, 0:1]),
               reads=[wk, "nmagic_t"], writes=[qk])
            npiece = width // 128
            gp_ = t % 2
            for pp_ in range(npiece):
                em("pe", lambda t_, t=t, pp_=pp_, gp_=gp_: t_.transpose(
                    psb[gp_][:, 128 * pp_:128 * (pp_ + 1)],
                    wqi[t % 2][:][:, 128 * pp_:128 * (pp_ + 1)],
                    idt_b[:]),
                   reads=[qk, "idt_b"], writes=[PSB[gp_]])
            em("dve", lambda v, gp_=gp_, npiece=npiece: v.tensor_copy(
                wqt[gp_][:][:, 0:128 * npiece], psb[gp_][:, 0:128 * npiece]),
               reads=[PSB[gp_]], writes=[f"wqt{gp_}"])
            em("qs", lambda s, out_h=out_h, i=i, off=off, gp_=gp_,
               npiece=npiece: s.dma_start(
                   out=out_h.rearrange("(a p) f -> p a f", p=128)
                   [:, off // 128:off // 128 + npiece,
                    128 * i:128 * (i + 1)],
                   in_=wqt[gp_][:][:, 0:128 * npiece].rearrange(
                       "p (a f) -> p a f", f=128)),
               reads=[f"wqt{gp_}"], dma=f"wt{gp_}")


        wq_bufs = {}
        wq_pos = [0]

        def wq_step(n=1):
            for _ in range(n):
                t = wq_pos[0]
                if t >= len(wq_rtasks):
                    return
                if t + 1 < len(wq_rtasks):
                    wq_bufs[t + 1] = round_load(t + 1)
                round_compute(t, wq_bufs.pop(t))
                wq_pos[0] += 1

        for c in range(EC):
            for gi in range(3):
                em("gp", lambda g, c=c, gi=gi: g.dma_start(
                    out=wqkv[c][:, 1024 * gi:1024 * (gi + 1)].rearrange(
                        "p (h t) -> p h t", h=16),
                    in_=wqkv_h[128 * c:128 * (c + 1), :].rearrange(
                        "p (h t) -> p h t", h=16)[:, :, 64 * gi:64 * (gi + 1)]),
                   writes=[f"wqkv{c}"], dma="wq")
        for tg in range(NG):
            for tc4 in range(4):
                load_x(4 * tg + tc4)
                make_haT(4 * tg + tc4, tc4)
            for j in range(EC):
                pi = j % 2
                pk = PSF[pi]
                for c in range(EC):
                    em("pe", lambda t, j=j, c=c, pi=pi: t.matmul(
                        psf[pi][:, :],
                        wqkv[c][:, 1024 + 128 * j:1024 + 128 * (j + 1)],
                        haT[c][:], start=(c == 0), stop=(c == EC - 1)),
                       reads=[f"wqkv{c}"] + haT_keys, writes=[pk])
                em("dve", lambda v, j=j, tg=tg, pi=pi: v.tensor_single_scalar(
                    KT[j][:, 512 * tg:512 * (tg + 1)], psf[pi][:, :],
                    kb_cols[:, j:j + 1], ALU.add),
                   reads=[pk, "kb_cols"], writes=[f"KT{j}"])
            for tc4 in range(4):
                tc = 4 * tg + tc4
                for half in range(2):
                    pi = (2 * tc4 + half) % 2
                    pk = PSF[pi]
                    for c in range(EC):
                        em("pe", lambda t, tc4=tc4, half=half, c=c, pi=pi:
                           t.matmul(
                               psf[pi][:, :],
                               haT[c][:, 128 * tc4:128 * (tc4 + 1)],
                               wqkv[c][:, 2048 + 512 * half:2048 + 512 * (half + 1)],
                               start=(c == 0), stop=False),
                           reads=[f"wqkv{c}", f"haT{c}"], writes=[pk])
                    em("pe", lambda t, half=half, pi=pi: t.matmul(
                        psf[pi][:, :], ones_row[0:1, 0:128],
                        vb_row[0:1, 512 * half:512 * (half + 1)],
                        start=False, stop=True),
                       reads=["vb_row", "ones_row"], writes=[pk])
                    em("dve", lambda v, tc=tc, half=half, pi=pi: v.tensor_copy(
                        V[tc][:, 520 * half:520 * (half + 1)].rearrange(
                            "p (h c) -> p h c", c=65)[:, :, 0:64],
                        psf[pi][:, :].rearrange("p (h c) -> p h c", c=64)),
                       reads=[pk], writes=[f"V{tc}"])
                em("dve", lambda v, tc=tc: v.memset(
                    V[tc][:, 0:1040].rearrange("p (h c) -> p h c",
                                               c=65)[:, :, 64:65], 1.0),
                   writes=[f"V{tc}"])
            for task in wq_tasks[13 * tg:13 * (tg + 1)]:
                scale_task(task)


        for c in range(EC):
            em("gp", lambda g, c=c: g.dma_start(
                out=wout[c][:], in_=wout_h[128 * c:128 * (c + 1), :]),
               writes=[f"wout{c}"], dma="wo")
        scale_finish()
        wq_bufs[0] = round_load(0)
        wq_pos = [0]
        for qg in range(NG):
            for tc4 in range(4):
                load_x(4 * qg + tc4)
                make_haT(4 * qg + tc4, tc4)
            def qproj(j):
                qb_ = j % 2
                for c in range(EC):
                    em("pe", lambda t, j=j, c=c: t.matmul(
                        psc[0][:, 0:512],
                        wqkv[c][:, 128 * j:128 * (j + 1)],
                        haT[c][:], start=(c == 0), stop=(c == EC - 1)),
                       reads=[f"wqkv{c}"] + haT_keys, writes=["psc0"])
                em("dve", lambda v, j=j, qb_=qb_: v.tensor_scalar(
                    QT[qb_][:], psc[0][:, 0:512], 0.125, qb_cols[:, j:j + 1],
                    ALU.mult, ALU.add),
                   reads=["psc0", "qb_cols"], writes=[f"QT{qb_}"])

            qproj(0)
            for j in range(EC):
                qb_ = j % 2

                def qk(kc, j=j, qb_=qb_):
                    par = kc % 2
                    for hh in range(2):
                        em("pe", lambda t, j=j, kc=kc, hh=hh, par=par,
                           qb_=qb_: t.matmul(
                               psc[par][:, 512 * hh:512 * (hh + 1)],
                               KT[j][64 * hh:64 * (hh + 1),
                                     128 * kc:128 * (kc + 1)],
                               QT[qb_][64 * hh:64 * (hh + 1), :],
                               start=True, stop=True),
                           reads=[f"KT{j}", f"QT{qb_}"], writes=[PSC[par]])

                def expv(kc):
                    par = kc % 2
                    em("act", lambda a, par=par: a.activation(
                        PT[par][:], psc[par][:, :], AF.Exp),
                       reads=[PSC[par]], writes=[f"PT{par}"])

                def pv(kc, j=j):
                    par = kc % 2
                    for hh in range(2):
                        em("pe", lambda t, j=j, kc=kc, hh=hh, par=par: t.matmul(
                            psf[hh][0:65, :],
                            V[kc][:, 65 * (2 * j + hh):65 * (2 * j + hh) + 65],
                            PT[par][:, 512 * hh:512 * (hh + 1)],
                            start=(kc == 0), stop=(kc == NT - 1)),
                           reads=[f"V{kc}", f"PT{par}"], writes=[PSF[hh]])

                qk(0)
                expv(0)
                for kc in range(1, NT):
                    qk(kc)
                    expv(kc)
                    pv(kc - 1)
                pv(NT - 1)
                if j + 1 < EC:
                    qproj(j + 1)
                for hh in range(2):
                    em("act", lambda a, hh=hh: a.activation(
                        recips[hh][0:1, :], psf[hh][64:65, :], AF.Ln),
                       reads=[PSF[hh]], writes=[f"recip{hh}"])
                    em("act", lambda a, hh=hh: a.activation(
                        recips[hh][0:1, :], recips[hh][0:1, :], AF.Exp,
                        scale=-1.0),
                       reads=[f"recip{hh}"], writes=[f"recip{hh}"])
                    em("pe", lambda t, hh=hh: t.matmul(
                        psc[hh][0:64, 0:512], onef_row[0:1, 0:64],
                        recips[hh][0:1, :], start=True, stop=True),
                       reads=[f"recip{hh}", "onef_row"], writes=[PSC[hh]])
                    em("dve", lambda v, hh=hh: v.tensor_copy(
                        recips[hh][0:64, :], psc[hh][0:64, 0:512]),
                       reads=[PSC[hh]], writes=[f"recip{hh}"])
                    em("dve", lambda v, j=j, hh=hh: v.tensor_mul(
                        attnT[j][64 * hh:64 * (hh + 1), :],
                        psf[hh][0:64, :], recips[hh][0:64, :]),
                       reads=[PSF[hh], f"recip{hh}"], writes=[f"attnT{j}"])
                wq_step(2)
            for tp in range(2):
                for half in range(2):
                    for j in range(EC):
                        for u in range(2):
                            tc4 = 2 * tp + u
                            em("pe", lambda t, j=j, tc4=tc4, u=u, half=half:
                               t.matmul(
                                   psc[u][:, 512 * half:512 * (half + 1)],
                                   attnT[j][:, 128 * tc4:128 * (tc4 + 1)],
                                   wout[j][:, 512 * half:512 * (half + 1)],
                                   start=(j == 0), stop=False),
                               reads=[f"attnT{j}", f"wout{j}"], writes=[PSC[u]])
                    for u in range(2):
                        em("pe", lambda t, u=u, half=half: t.matmul(
                            psc[u][:, 512 * half:512 * (half + 1)],
                            ones_row[0:1, 0:128],
                            bout_row[0:1, 512 * half:512 * (half + 1)],
                            start=False, stop=True),
                           reads=["bout_row", "ones_row"], writes=[PSC[u]])
                for u in range(2):
                    tc4 = 2 * tp + u
                    tc = 4 * qg + tc4
                    xk = load_x(tc)
                    for half in range(2):
                        em("dve", lambda v, u=u, half=half: v.tensor_mul(
                            hbst[u][:, 512 * half:512 * (half + 1)],
                            psc[u][:, 512 * half:512 * (half + 1)],
                            ga_b[:, 512 * half:512 * (half + 1)]),
                           reads=[PSC[u], "ga_b"], writes=[f"hbst{u}"])
                    em("dve", lambda v, u=u, tc=tc: v.tensor_add(
                        hbst[u][:], hbst[u][:], xs[tc % 2][:]),
                       reads=[f"hbst{u}", xk], writes=[f"hbst{u}"])
                    em("qs", lambda s, tc=tc, u=u: s.dma_start(
                        out=x2_h[128 * tc:128 * (tc + 1), :], in_=hbst[u][:]),
                       reads=[f"hbst{u}"], dma=f"sx2{u}")
                wq_step(1)



        for colv, ck, dstx, dk in ((mod_opcb, "mod_opcb", opcb_b, "xs0"),
                                   (mod_sb, "mod_sb", sb_b, "xs1")):
            for c in range(EC):
                em("pe", lambda t, colv=colv, c=c: t.transpose(
                    psc[0][0:1, 128 * c:128 * (c + 1)],
                    colv[:, c:c + 1], idt_f[:]),
                   reads=[ck, "idt_f"], writes=["psc0"])
            em("dve", lambda v, dstx=dstx: v.tensor_copy(
                dstx[0:1, 0:E], psc[0][0:1, 0:E]),
               reads=["psc0"], writes=[dk])
            for hf in range(2):
                em("pe", lambda t, dstx=dstx, hf=hf: t.matmul(
                    psc[1][:, 0:512], onef_row[0:1, 0:128],
                    dstx[0:1, 512 * hf:512 * (hf + 1)],
                    start=True, stop=True),
                   reads=[dk, "onef_row"], writes=["psc1"])
                em("dve", lambda v, dstx=dstx, hf=hf: v.tensor_copy(
                    dstx[:, 512 * hf:512 * (hf + 1)], psc[1][:, 0:512]),
                   reads=["psc1"], writes=[dk])
        for tc in range(NT):
            hb_i = tc % 2
            hk = f"hbst{hb_i}"
            em("qs", lambda s, tc=tc, hb_i=hb_i: s.dma_start(
                out=hbst[hb_i][:], in_=x2_h[128 * tc:128 * (tc + 1), :]),
               writes=[hk], dma=f"sx2{hb_i}")
            em("act", lambda a, tc=tc, hb_i=hb_i: a.activation(
                normed[:], hbst[hb_i][:], AF.Square,
                accum_out=s2c[:, 0:1]),
               reads=[hk], writes=["normed", "s2c0"])
            em("dve", lambda v, hb_i=hb_i: v.reduce_sum(
                s2c[:, 1:2], hbst[hb_i][:], AX.X),
               reads=[hk], writes=["s2c1"])
            em("dve", lambda v: v.tensor_scalar_mul(
                s2c[:, 2:3], s2c[:, 1:2], -1.0 / E),
               reads=["s2c1"], writes=["s2c2"])
            em("dve", lambda v: v.tensor_scalar_mul(
                s2c[:, 3:4], s2c[:, 0:1], 1.0 / E),
               reads=["s2c0"], writes=["s2c3"])
            em("dve", lambda v: v.tensor_mul(
                s2c[:, 4:5], s2c[:, 2:3], s2c[:, 2:3]),
               reads=["s2c2"], writes=["s2c4"])
            em("dve", lambda v: v.tensor_sub(
                s2c[:, 3:4], s2c[:, 3:4], s2c[:, 4:5]),
               reads=["s2c3", "s2c4"], writes=["s2c3"])
            em("act", lambda a: a.activation(
                s2c[:, 4:5], s2c[:, 3:4], AF.Sqrt, bias=eps_t[:, 0:1]),
               reads=["s2c3", "eps_t"], writes=["s2c4"])
            em("dve", lambda v: v.reciprocal(s2c[:, 5:6], s2c[:, 4:5]),
               reads=["s2c4"], writes=["s2c5"])
            em("dve", lambda v, hb_i=hb_i: v.tensor_scalar(
                wnat[hb_i][:][:, 0:E], hbst[hb_i][:], s2c[:, 2:3], s2c[:, 5:6],
                ALU.add, ALU.mult),
               reads=[hk, "s2c2", "s2c5"], writes=[f"wnat{hb_i}"])
            em("gp", lambda g, hb_i=hb_i: g.tensor_mul(
                wnat[hb_i][:][:, 0:E], wnat[hb_i][:][:, 0:E], opcb_b[:]),
               reads=[f"wnat{hb_i}", "xs0"], writes=[f"wnat{hb_i}"])
            em("gp", lambda g, hb_i=hb_i: g.tensor_add(
                wnat[hb_i][:][:, 0:E], wnat[hb_i][:][:, 0:E], sb_b[:]),
               reads=[f"wnat{hb_i}", "xs1"], writes=[f"wnat{hb_i}"])
            em("dve", lambda v, hb_i=hb_i: v.tensor_reduce(
                sc_t[:, 0:1], wnat[hb_i][:][:, 0:E], AX.X, ALU.max,
                apply_absolute_value=True),
               reads=[f"wnat{hb_i}"], writes=["sc0"])
            em("dve", lambda v: v.tensor_max(m1run[:], m1run[:],
                                             sc_t[:, 0:1]),
               reads=["m1run", "sc0"], writes=["m1run"])
            em("qs", lambda s, tc=tc, hb_i=hb_i: s.dma_start(
                out=hb_h[128 * tc:128 * (tc + 1), :], in_=wnat[hb_i][:][:, 0:E]),
               reads=[f"wnat{hb_i}"], dma=f"shb{hb_i}")

        em("pe", lambda t: t.transpose(psf[0][0:1, 0:128], m1run[:], idt_f[:]),
           reads=["m1run", "idt_f"], writes=["ps0"])
        em("dve", lambda v: v.tensor_reduce(sc_t[0:1, 1:2],
                                            psf[0][0:1, 0:128], AX.X, ALU.max),
           reads=["ps0"], writes=["sc1"])
        em("qs", lambda s: s.dma_start(out=m1_h[:, :], in_=sc_t[0:1, 1:2]),
           reads=["sc1"], dma="sm")
        while wq_pos[0] < len(wq_rtasks):
            wq_step(1)



        sems = {n: ctx.enter_context(nc.semaphore(n)) for n in p.cum}
        block = ctx.enter_context(nc.Block())
        p.finish(block, sems)
    return nc





def build_l2():
    nc = bass.Bass("TRN2", target_bir_lowering=False)
    p = Prog()

    hb_h = nc.dram_tensor("hb", [S, E], F32, kind="ExternalInput")
    m1v_h = nc.dram_tensor("m1vec", [1, NCORES], F32, kind="ExternalInput")
    clipx_h = nc.dram_tensor("clipx", [1, 1], F32, kind="ExternalInput")
    wgq_h = nc.dram_tensor("wgqT", [E, FI], BF16, kind="ExternalInput")
    wuq_h = nc.dram_tensor("wuqT", [E, FI], BF16, kind="ExternalInput")
    sg_h = nc.dram_tensor("sg", [128, IC], F32, kind="ExternalInput")
    su_h = nc.dram_tensor("su", [128, IC], F32, kind="ExternalInput")

    g_h = nc.dram_tensor("g", [FI, S], F32, kind="ExternalOutput")
    m2_h = nc.dram_tensor("m2loc", [1, 1], F32, kind="ExternalOutput")

    with ExitStack() as ctx:
        ab_t = ctx.enter_context(nc.sbuf_tensor("ab", [128, 64000], BF16))
        af_t = ctx.enter_context(nc.sbuf_tensor("af", [128, 8000], F32))
        AB = Arena(ab_t, 64000)
        AFA = Arena(af_t, 8000)
        em = p.emit

        wgq = AB.alloc_n(EC, 128, FI)
        wuq = AB.alloc_n(EC, 128, FI)
        xqT = AB.alloc_n(EC, 128, S)
        xqs = AB.alloc_n(2, 128, E)
        idt_b = AB.alloc(128, 128)

        hbs = AFA.alloc_n(2, 128, E)
        silu_sb = AFA.alloc_n(2, 128, 512)
        g_sb = AFA.alloc_n(2, 128, 512)
        sg_t = AFA.alloc(128, IC)
        su_t = AFA.alloc(128, IC)
        sgeff = AFA.alloc(128, IC)
        sueff = AFA.alloc(128, IC)
        m1v = AFA.alloc(1, NCORES)
        clipx = AFA.alloc(1, 1)
        scal = AFA.alloc(1, 4)
        inv1b = AFA.alloc(128, 1)
        s1b = AFA.alloc(128, 1)
        magic_t = AFA.alloc(128, 1)
        m2run = AFA.alloc(128, 1)
        sc_t = AFA.alloc(128, 2)
        onef_row = AFA.alloc(1, 128)
        idt_f = AFA.alloc(128, 128)
        iosc = AFA.alloc(128, 128)

        psf = [ctx.enter_context(nc.psum_tensor(f"pf{i}", [128, 512], F32))
               for i in range(6)]
        psb = [ctx.enter_context(nc.psum_tensor(f"pb{i}", [128, 1024], BF16))
               for i in range(2)]
        PSF = [f"ps{i}" for i in range(6)]

        em("gp", lambda g: g.iota(iosc[:], [[1, 128]], channel_multiplier=-1,
                                  allow_small_or_imprecise_dtypes=True),
           writes=["iosc"])
        em("dve", lambda v: v.tensor_single_scalar(idt_b[:], iosc[:], 0.0,
                                                   ALU.is_equal),
           reads=["iosc"], writes=["idt_b"])
        em("dve", lambda v: v.tensor_single_scalar(idt_f[:], iosc[:], 0.0,
                                                   ALU.is_equal),
           reads=["iosc"], writes=["idt_f"])
        em("dve", lambda v: v.memset(onef_row[:], 1.0), writes=["onef_row"])
        em("dve", lambda v: v.memset(magic_t[:], MAGIC), writes=["magic_t"])
        em("dve", lambda v: v.memset(m2run[:], 0.0), writes=["m2run"])

        em("qs", lambda s: s.dma_start(out=m1v[:], in_=m1v_h[:, :]),
           writes=["m1v"], dma="c")
        em("qs", lambda s: s.dma_start(out=clipx[:], in_=clipx_h[:, :]),
           writes=["clipx"], dma="c")
        em("qs", lambda s: s.dma_start(out=sg_t[:], in_=sg_h[:, :]),
           writes=["sg_t"], dma="c")
        em("qs", lambda s: s.dma_start(out=su_t[:], in_=su_h[:, :]),
           writes=["su_t"], dma="c")
        for c in range(EC):
            em("gp", lambda g, c=c: g.dma_start(
                out=wgq[c][:], in_=wgq_h[128 * c:128 * (c + 1), :]),
               writes=[f"wgq{c}"], dma="wg")
            em("gp", lambda g, c=c: g.dma_start(
                out=wuq[c][:], in_=wuq_h[128 * c:128 * (c + 1), :]),
               writes=[f"wuq{c}"], dma="wg")

        em("dve", lambda v: v.tensor_reduce(scal[0:1, 0:1], m1v[:], AX.X,
                                            ALU.max),
           reads=["m1v"], writes=["s_m1"])
        em("act", lambda a: a.activation(scal[0:1, 1:2], clipx[:], AF.Sigmoid),
           reads=["clipx"], writes=["s_sig"])
        em("dve", lambda v: v.tensor_mul(scal[0:1, 0:1], scal[0:1, 0:1],
                                         scal[0:1, 1:2]),
           reads=["s_m1", "s_sig"], writes=["s_s1"])
        em("dve", lambda v: v.reciprocal(scal[0:1, 2:3], scal[0:1, 0:1]),
           reads=["s_s1"], writes=["s_inv"])
        em("dve", lambda v: v.tensor_scalar_mul(scal[0:1, 2:3],
                                                scal[0:1, 2:3], 127.0),
           reads=["s_inv"], writes=["s_inv"])
        em("dve", lambda v: v.tensor_scalar_mul(scal[0:1, 3:4],
                                                scal[0:1, 0:1], 1.0 / 127.0),
           reads=["s_s1"], writes=["s_s1d"])
        for (cidx, dst, dk, rk) in ((2, inv1b, "inv1b", "s_inv"),
                                    (3, s1b, "s1b", "s_s1d")):
            em("pe", lambda t, cidx=cidx: t.matmul(
                psf[0][:, 0:1], onef_row[0:1, 0:128],
                scal[0:1, cidx:cidx + 1], start=True, stop=True),
               reads=[rk, "onef_row"], writes=["ps0"])
            em("dve", lambda v, dst=dst: v.tensor_copy(dst[:], psf[0][:, 0:1]),
               reads=["ps0"], writes=[dk])
        em("dve", lambda v: v.tensor_scalar(sgeff[:], sg_t[:], s1b[:, 0:1],
                                            None, ALU.mult),
           reads=["sg_t", "s1b"], writes=["sgeff"])
        em("dve", lambda v: v.tensor_scalar(sueff[:], su_t[:], s1b[:, 0:1],
                                            None, ALU.mult),
           reads=["su_t", "s1b"], writes=["sueff"])

        for tc in range(NT):
            hk = f"hbs{tc % 2}"
            em("qs", lambda s, tc=tc: s.dma_start(
                out=hbs[tc % 2][:], in_=hb_h[128 * tc:128 * (tc + 1), :]),
               writes=[hk], dma=f"x{tc % 2}")
            em("act", lambda a, tc=tc: a.activation(
                hbs[tc % 2][:], hbs[tc % 2][:], AF.Identity,
                scale=inv1b[:, 0:1], bias=magic_t[:, 0:1]),
               reads=[hk, "inv1b", "magic_t"], writes=[hk])
            em("dve", lambda v, tc=tc: v.tensor_scalar(
                hbs[tc % 2][:], hbs[tc % 2][:], MAGIC, 127.0,
                ALU.subtract, ALU.min),
               reads=[hk], writes=[hk])
            em("dve", lambda v, tc=tc: v.tensor_scalar_max(
                xqs[tc % 2][:], hbs[tc % 2][:], -128.0),
               reads=[hk], writes=[f"xqs{tc % 2}"])
            for c in range(EC):
                em("pe", lambda t, tc=tc, c=c: t.transpose(
                    psb[tc % 2][:, 128 * c:128 * (c + 1)],
                    xqs[tc % 2][:][:, 128 * c:128 * (c + 1)], idt_b[:]),
                   reads=[f"xqs{tc % 2}", "idt_b"], writes=[f"psb{tc % 2}"])
            for c in range(EC):
                em("dve", lambda v, tc=tc, c=c: v.tensor_copy(
                    xqT[c][:, 128 * tc:128 * (tc + 1)],
                    psb[tc % 2][:, 128 * c:128 * (c + 1)]),
                   reads=[f"psb{tc % 2}"], writes=[f"xqT{c}"])

        xqT_keys = [f"xqT{c}" for c in range(EC)]
        for i in range(IC):
            for tg in range(NG):
                pg = tg % 2
                pu = 2 + tg % 2
                for c in range(EC):
                    em("pe", lambda t, i=i, tg=tg, c=c, pg=pg: t.matmul(
                        psf[pg][:, :], wgq[c][:, 128 * i:128 * (i + 1)],
                        xqT[c][:, 512 * tg:512 * (tg + 1)],
                        start=(c == 0), stop=(c == EC - 1)),
                       reads=[f"wgq{c}"] + xqT_keys, writes=[PSF[pg]])
                for c in range(EC):
                    em("pe", lambda t, i=i, tg=tg, c=c, pu=pu: t.matmul(
                        psf[pu][:, :], wuq[c][:, 128 * i:128 * (i + 1)],
                        xqT[c][:, 512 * tg:512 * (tg + 1)],
                        start=(c == 0), stop=(c == EC - 1)),
                       reads=[f"wuq{c}"] + xqT_keys, writes=[PSF[pu]])
                em("act", lambda a, i=i, tg=tg, pg=pg: a.activation(
                    silu_sb[tg % 2][:], psf[pg][:, :], AF.Silu,
                    scale=sgeff[:, i:i + 1]),
                   reads=[PSF[pg], "sgeff"], writes=[f"silu{tg % 2}"])
                em("dve", lambda v, tg=tg, pu=pu: v.tensor_mul(
                    g_sb[tg % 2][:], silu_sb[tg % 2][:], psf[pu][:, :]),
                   reads=[f"silu{tg % 2}", PSF[pu]], writes=[f"g{tg % 2}"])
                em("dve", lambda v, tg=tg: v.tensor_reduce(
                    sc_t[:, 0:1], g_sb[tg % 2][:], AX.X, ALU.max,
                    apply_absolute_value=True),
                   reads=[f"g{tg % 2}"], writes=["sc0"])
                em("dve", lambda v, i=i: v.tensor_scalar(
                    sc_t[:, 1:2], sc_t[:, 0:1], sueff[:, i:i + 1], None,
                    ALU.mult),
                   reads=["sc0", "sueff"], writes=["sc1"])
                em("dve", lambda v: v.tensor_max(m2run[:], m2run[:],
                                                 sc_t[:, 1:2]),
                   reads=["m2run", "sc1"], writes=["m2run"])
                em("qs", lambda s, i=i, tg=tg: s.dma_start(
                    out=g_h[128 * i:128 * (i + 1), 512 * tg:512 * (tg + 1)],
                    in_=g_sb[tg % 2][:]),
                   reads=[f"g{tg % 2}"], dma=f"sg{tg % 2}")

        em("pe", lambda t: t.transpose(psf[0][0:1, 0:128], m2run[:], idt_f[:]),
           reads=["m2run", "idt_f"], writes=["ps0"])
        em("dve", lambda v: v.tensor_reduce(sc_t[0:1, 0:1],
                                            psf[0][0:1, 0:128], AX.X, ALU.max),
           reads=["ps0"], writes=["sc0f"])
        em("qs", lambda s: s.dma_start(out=m2_h[:, :], in_=sc_t[0:1, 0:1]),
           reads=["sc0f"], dma="sm")

        sems = {n: ctx.enter_context(nc.semaphore(n)) for n in p.cum}
        block = ctx.enter_context(nc.Block())
        p.finish(block, sems)
    return nc



def build_l3():
    nc = bass.Bass("TRN2", target_bir_lowering=False)
    p = Prog()

    g_h = nc.dram_tensor("g", [FI, S], F32, kind="ExternalInput")
    m1v_h = nc.dram_tensor("m1vec", [1, NCORES], F32, kind="ExternalInput")
    m2v_h = nc.dram_tensor("m2vec", [1, NCORES], F32, kind="ExternalInput")
    clipx_h = nc.dram_tensor("clipx", [1, 1], F32, kind="ExternalInput")
    clipd_h = nc.dram_tensor("clipd", [1, 1], F32, kind="ExternalInput")
    su_h = nc.dram_tensor("su", [128, IC], F32, kind="ExternalInput")
    wdq_h = nc.dram_tensor("wdqT", [FI, E], BF16, kind="ExternalInput")
    sd_h = nc.dram_tensor("sd", [128, EC], F32, kind="ExternalInput")
    x2_h = nc.dram_tensor("x2", [S, E], F32, kind="ExternalInput")
    gbp_h = nc.dram_tensor("gbp", [1, E], F32, kind="ExternalInput")

    out_h = nc.dram_tensor("out", [S, E], F32, kind="ExternalOutput")

    with ExitStack() as ctx:
        ab_t = ctx.enter_context(nc.sbuf_tensor("ab", [128, 36000], BF16))
        af_t = ctx.enter_context(nc.sbuf_tensor("af", [128, 9000], F32))
        AB = Arena(ab_t, 36000)
        AFA = Arena(af_t, 9000)
        em = p.emit

        wdq = AB.alloc_n(IC, 128, E)
        hqT = AB.alloc_n(IC, 128, 512)
        idt_b = AB.alloc(128, 128)

        gst = AFA.alloc_n(2, 128, 512)
        x2s = AFA.alloc_n(2, 128, E)
        outs = AFA.alloc_n(2, 128, E)
        F_b = AFA.alloc(128, E)
        F_row = AFA.alloc(1, E)
        sd_t = AFA.alloc(128, EC)
        su_t = AFA.alloc(128, IC)
        qf = AFA.alloc(128, IC)
        m1v = AFA.alloc(1, NCORES)
        m2v = AFA.alloc(1, NCORES)
        clipx = AFA.alloc(1, 1)
        clipd = AFA.alloc(1, 1)
        gbp = AFA.alloc(1, E)
        scal = AFA.alloc(1, 6)
        inv2b = AFA.alloc(128, 1)
        magic_t = AFA.alloc(128, 1)
        onef_row = AFA.alloc(1, 128)
        idt_f = AFA.alloc(128, 128)
        iosc = AFA.alloc(128, 128)

        psf = [ctx.enter_context(nc.psum_tensor(f"pf{i}", [128, 512], F32))
               for i in range(4)]
        PSF = [f"ps{i}" for i in range(4)]

        em("gp", lambda g: g.iota(iosc[:], [[1, 128]], channel_multiplier=-1,
                                  allow_small_or_imprecise_dtypes=True),
           writes=["iosc"])
        em("dve", lambda v: v.tensor_single_scalar(idt_b[:], iosc[:], 0.0,
                                                   ALU.is_equal),
           reads=["iosc"], writes=["idt_b"])
        em("dve", lambda v: v.tensor_single_scalar(idt_f[:], iosc[:], 0.0,
                                                   ALU.is_equal),
           reads=["iosc"], writes=["idt_f"])
        em("dve", lambda v: v.memset(onef_row[:], 1.0), writes=["onef_row"])
        em("dve", lambda v: v.memset(magic_t[:], MAGIC), writes=["magic_t"])

        for nm, dst, hh in (("m1v", m1v, m1v_h), ("m2v", m2v, m2v_h),
                            ("clipx", clipx, clipx_h), ("clipd", clipd, clipd_h),
                            ("su_t", su_t, su_h), ("sd_t", sd_t, sd_h),
                            ("gbp", gbp, gbp_h)):
            em("qs", lambda s, dst=dst, hh=hh: s.dma_start(out=dst[:],
                                                           in_=hh[:, :]),
               writes=[nm], dma="c")
        for i in range(IC):
            em("gp", lambda g, i=i: g.dma_start(
                out=wdq[i][:], in_=wdq_h[128 * i:128 * (i + 1), :]),
               writes=[f"wdq{i}"], dma="wd")

        em("dve", lambda v: v.tensor_reduce(scal[0:1, 0:1], m1v[:], AX.X,
                                            ALU.max),
           reads=["m1v"], writes=["s_m1"])
        em("act", lambda a: a.activation(scal[0:1, 1:2], clipx[:], AF.Sigmoid),
           reads=["clipx"], writes=["s_sx"])
        em("dve", lambda v: v.tensor_mul(scal[0:1, 0:1], scal[0:1, 0:1],
                                         scal[0:1, 1:2]),
           reads=["s_m1", "s_sx"], writes=["s_s1"])
        em("dve", lambda v: v.tensor_reduce(scal[0:1, 2:3], m2v[:], AX.X,
                                            ALU.max),
           reads=["m2v"], writes=["s_m2"])
        em("act", lambda a: a.activation(scal[0:1, 3:4], clipd[:], AF.Sigmoid),
           reads=["clipd"], writes=["s_sd"])
        em("dve", lambda v: v.tensor_mul(scal[0:1, 2:3], scal[0:1, 2:3],
                                         scal[0:1, 3:4]),
           reads=["s_m2", "s_sd"], writes=["s_s2m"])
        em("dve", lambda v: v.reciprocal(scal[0:1, 4:5], scal[0:1, 2:3]),
           reads=["s_s2m"], writes=["s_i2"])
        em("dve", lambda v: v.tensor_scalar_mul(scal[0:1, 5:6],
                                                scal[0:1, 2:3], 1.0 / 127.0),
           reads=["s_s2m"], writes=["s_s2"])
        em("dve", lambda v: v.tensor_mul(scal[0:1, 0:1], scal[0:1, 0:1],
                                         scal[0:1, 4:5]),
           reads=["s_s1", "s_i2"], writes=["s_q"])
        em("pe", lambda t: t.matmul(psf[0][:, 0:1], onef_row[0:1, 0:128],
                                    scal[0:1, 0:1], start=True, stop=True),
           reads=["s_q", "onef_row"], writes=["ps0"])
        em("dve", lambda v: v.tensor_copy(inv2b[:], psf[0][:, 0:1]),
           reads=["ps0"], writes=["inv2b"])
        em("dve", lambda v: v.tensor_scalar(qf[:], su_t[:], inv2b[:, 0:1],
                                            None, ALU.mult),
           reads=["su_t", "inv2b"], writes=["qf"])
        for c in range(EC):
            em("pe", lambda t, c=c: t.transpose(
                psf[0][0:1, 128 * c:128 * (c + 1) - 384] if False else
                psf[0][0:1, 0:128],
                sd_t[:, c:c + 1], idt_f[:]),
               reads=["sd_t", "idt_f"], writes=["ps0"])
            em("dve", lambda v, c=c: v.tensor_copy(
                F_row[0:1, 128 * c:128 * (c + 1)], psf[0][0:1, 0:128]),
               reads=["ps0"], writes=["F_row"])
        em("dve", lambda v: v.tensor_mul(F_row[:], F_row[:], gbp[:]),
           reads=["F_row", "gbp"], writes=["F_row"])
        em("dve", lambda v: v.tensor_scalar(F_row[:], F_row[:],
                                            scal[0:1, 5:6], None, ALU.mult),
           reads=["F_row", "s_s2"], writes=["F_row"])
        for hf in range(2):
            em("pe", lambda t, hf=hf: t.matmul(
                psf[1][:, :], onef_row[0:1, 0:128],
                F_row[0:1, 512 * hf:512 * (hf + 1)], start=True, stop=True),
               reads=["F_row", "onef_row"], writes=["ps1"])
            em("dve", lambda v, hf=hf: v.tensor_copy(
                F_b[:, 512 * hf:512 * (hf + 1)], psf[1][:, :]),
               reads=["ps1"], writes=["F_b"])

        for tg in range(NG):
            for i in range(IC):
                gk = f"gst{i % 2}"
                em("qs", lambda s, i=i, tg=tg: s.dma_start(
                    out=gst[i % 2][:],
                    in_=g_h[128 * i:128 * (i + 1), 512 * tg:512 * (tg + 1)]),
                   writes=[gk], dma=f"g{i % 2}")
                em("act", lambda a, i=i: a.activation(
                    gst[i % 2][:], gst[i % 2][:], AF.Identity,
                    scale=qf[:, i:i + 1], bias=magic_t[:, 0:1]),
                   reads=[gk, "qf", "magic_t"], writes=[gk])
                em("dve", lambda v, i=i: v.tensor_scalar(
                    gst[i % 2][:], gst[i % 2][:], MAGIC, 127.0,
                    ALU.subtract, ALU.min),
                   reads=[gk], writes=[gk])
                em("dve", lambda v, i=i: v.tensor_scalar_max(
                    hqT[i][:], gst[i % 2][:], -128.0),
                   reads=[gk], writes=[f"hqT{i}"])
            for tc4 in range(4):
                tc = 4 * tg + tc4
                xk = f"x2s{tc % 2}"
                em("qs", lambda s, tc=tc: s.dma_start(
                    out=x2s[tc % 2][:], in_=x2_h[128 * tc:128 * (tc + 1), :]),
                   writes=[xk], dma=f"x{tc % 2}")
                for hg in range(2):
                    pi = hg
                    pk = PSF[pi]
                    for i in range(IC):
                        em("pe", lambda t, i=i, tc4=tc4, hg=hg, pi=pi: t.matmul(
                            psf[pi][:, :],
                            hqT[i][:, 128 * tc4:128 * (tc4 + 1)],
                            wdq[i][:, 512 * hg:512 * (hg + 1)],
                            start=(i == 0), stop=(i == IC - 1)),
                           reads=[f"hqT{i}", f"wdq{i}"], writes=[pk])
                    em("dve", lambda v, tc=tc, hg=hg, pi=pi: v.tensor_mul(
                        outs[tc % 2][:][:, 512 * hg:512 * (hg + 1)],
                        psf[pi][:, :], F_b[:, 512 * hg:512 * (hg + 1)]),
                       reads=[pk, "F_b"], writes=[f"outs{tc % 2}"])
                    em("dve", lambda v, tc=tc, hg=hg: v.tensor_add(
                        outs[tc % 2][:][:, 512 * hg:512 * (hg + 1)],
                        outs[tc % 2][:][:, 512 * hg:512 * (hg + 1)],
                        x2s[tc % 2][:][:, 512 * hg:512 * (hg + 1)]),
                       reads=[f"outs{tc % 2}", xk], writes=[f"outs{tc % 2}"])
                em("qs", lambda s, tc=tc: s.dma_start(
                    out=out_h[128 * tc:128 * (tc + 1), :], in_=outs[tc % 2][:]),
                   reads=[f"outs{tc % 2}"], dma=f"so{tc % 2}")

        sems = {n: ctx.enter_context(nc.semaphore(n)) for n in p.cum}
        block = ctx.enter_context(nc.Block())
        p.finish(block, sems)
    return nc



_CACHE = {}


def _programs():
    if "l1" not in _CACHE:
        _CACHE["l1"] = build_l1()
        _CACHE["l2"] = build_l2()
        _CACHE["l3"] = build_l3()
    return _CACHE["l1"], _CACHE["l2"], _CACHE["l3"]


def kernel(L0_attn_x, L0_attn_Wqkv, L0_attn_bqkv, L0_attn_Wout, L0_attn_bout,
           L0_ffn_clip_x, L0_ffn_clip_down, L0_ffn_w_gate, L0_ffn_w_up,
           L0_ffn_w_down, L0_c, L0_wmod):
    f32 = lambda a: np.ascontiguousarray(np.asarray(a, dtype=np.float32))
    x = f32(L0_attn_x)
    wqkv = f32(L0_attn_Wqkv)
    bqkv = f32(L0_attn_bqkv)
    wout = f32(L0_attn_Wout)
    bout = f32(L0_attn_bout).reshape(1, E)
    cmat = f32(L0_c)
    wmod = f32(L0_wmod)
    wg = f32(L0_ffn_w_gate)
    wu = f32(L0_ffn_w_up)
    wd = f32(L0_ffn_w_down)
    clipx = f32(L0_ffn_clip_x).reshape(1, 1)
    clipd = f32(L0_ffn_clip_down).reshape(1, 1)

    jj = np.arange(EC)
    pp = np.arange(128)
    qb = np.ascontiguousarray(
        bqkv[384 * jj[None, :] + 192 * (pp[:, None] // 64) + (pp[:, None] % 64)])
    kb = np.ascontiguousarray(
        bqkv[384 * jj[None, :] + 64 + 192 * (pp[:, None] // 64)
             + (pp[:, None] % 64)])
    hh = np.arange(16)
    dd = np.arange(64)
    vb = np.ascontiguousarray(
        bqkv[192 * hh[:, None] + 128 + dd[None, :]].reshape(1, E))

    l1, l2, l3 = _programs()

    in1 = [{
        "x": np.ascontiguousarray(x[b]),
        "wqkv": wqkv, "qb": qb, "kb": kb, "vb": vb,
        "wout": wout, "bout": bout,
        "cvec": np.ascontiguousarray(cmat[b].reshape(EC, 128)),
        "wmod": wmod, "wg": wg, "wu": wu, "wd": wd,
    } for b in range(NCORES)]
    r1 = run_bass_kernel_spmd(l1, in1, core_ids=list(range(NCORES))).results

    m1vec = np.ascontiguousarray(
        np.stack([r1[b]["m1loc"][0, 0] for b in range(NCORES)])[None, :])
    in2 = [{
        "hb": r1[b]["hb"], "m1vec": m1vec, "clipx": clipx,
        "wgqT": r1[b]["wgqT"], "wuqT": r1[b]["wuqT"],
        "sg": r1[b]["sg"], "su": r1[b]["su"],
    } for b in range(NCORES)]
    r2 = run_bass_kernel_spmd(l2, in2, core_ids=list(range(NCORES))).results

    m2vec = np.ascontiguousarray(
        np.stack([r2[b]["m2loc"][0, 0] for b in range(NCORES)])[None, :])
    in3 = [{
        "g": r2[b]["g"], "m1vec": m1vec, "m2vec": m2vec,
        "clipx": clipx, "clipd": clipd, "su": r1[b]["su"],
        "wdqT": r1[b]["wdqT"], "sd": r1[b]["sd"],
        "x2": r1[b]["x2"], "gbp": r1[b]["gbp"],
    } for b in range(NCORES)]
    r3 = run_bass_kernel_spmd(l3, in3, core_ids=list(range(NCORES))).results

    out = np.stack([r3[b]["out"] for b in range(NCORES)], axis=0)
    return out.astype(np.float32)



# version 12
# speedup vs baseline: 1.0307x; 1.0307x over previous
from contextlib import ExitStack

import numpy as np

import concourse.bass as bass
import concourse.mybir as mybir
from concourse.bass_utils import run_bass_kernel_spmd

F32 = mybir.dt.float32
BF16 = mybir.dt.bfloat16
AF = mybir.ActivationFunctionType
ALU = mybir.AluOpType
AX = mybir.AxisListType

MAGIC = float(np.float32(1.5 * 2 ** 23))
NCORES = 8
S = 2048
E = 1024
FI = 2816
NT = S // 128
NG = 4
EC = E // 128
IC = FI // 128
EPS = 1e-5

ENGINES = ("pe", "act", "dve", "gp", "qs")


class Prog:

    def __init__(self):
        self.items = {e: [] for e in ENGINES}
        self.cum = {s: 0 for s in ("pe", "act", "dve", "gp_c")}
        self.state = {}
        self.waited = {e: {} for e in ENGINES}

    def _sem_for(self, stream, dma):
        if dma:
            return ("qd_" + dma, 16)
        if stream == "gp":
            return ("gp_c", 1)
        assert stream != "qs", "qs stream is DMA-only"
        return (stream, 1)

    def emit(self, stream, fn, reads=(), writes=(), dma=None):
        sem, inc = self._sem_for(stream, dma)
        if sem not in self.cum:
            self.cum[sem] = 0
        want = {}
        own = self.waited[stream]

        def need(dep):
            if dep is None:
                return
            dsem, dval = dep
            if dsem == "pe" and stream == "pe":
                return
            if dsem.startswith("qd_"):
                dval = self.cum[dsem]
            if own.get(dsem, 0) >= dval:
                return
            want[dsem] = max(want.get(dsem, 0), dval)

        for k in reads:
            st = self.state.get(k)
            if st:
                need(st["w"])
        for k in writes:
            st = self.state.get(k)
            if st:
                need(st["w"])
                for r in st["r"]:
                    need(r)
        waits = sorted(want.items())
        for dsem, dval in waits:
            own[dsem] = dval
        self.items[stream].append((waits, fn, sem, inc))
        self.cum[sem] += inc
        val = (sem, self.cum[sem])
        for k in reads:
            st = self.state.setdefault(k, {"w": None, "r": []})
            st["r"].append(val)
        for k in writes:
            self.state[k] = {"w": val, "r": []}
        return val

    def finish(self, block, sems):
        finals = dict(self.cum)

        def make_runner(stream, tail=False):
            def run(eng):
                for waits, fn, sem, inc in self.items[stream]:
                    for dsem, dval in waits:
                        eng.wait_ge(sems[dsem], dval)
                    fn(eng).then_inc(sems[sem], inc)
                if tail:
                    for dsem, fv in finals.items():
                        if dsem.startswith("qd_") and fv:
                            eng.wait_ge(sems[dsem], fv)
            return run

        block.sync(make_runner("qs", tail=True))
        if self.items["gp"]:
            block.gpsimd(make_runner("gp"))
        if self.items["pe"]:
            block.tensor(make_runner("pe"))
        if self.items["dve"]:
            block.vector(make_runner("dve"))
        if self.items["act"]:
            block.scalar(make_runner("act"))


class View:
    __slots__ = ("ap", "off")

    def __init__(self, ap, off):
        self.ap = ap
        self.off = off

    def __getitem__(self, k):
        return self.ap[k]


class Arena:
    def __init__(self, handle, width):
        self.h = handle
        self.w = width
        self.off = 0

    def alloc(self, parts, width):
        assert self.off + width <= self.w, f"arena overflow {self.off}+{width}"
        v = View(self.h[0:parts, self.off:self.off + width], self.off)
        self.off += width
        return v

    def alloc_n(self, n, parts, width):
        return [self.alloc(parts, width) for _ in range(n)]

    def seg_ap(self, view, col, seg_step, nseg, inner, parts=128):
        return bass.AP(self.h, view.off + col,
                       [[self.w, parts], [seg_step, nseg], [1, inner]])



def build_l1():
    nc = bass.Bass("TRN2", target_bir_lowering=False)
    p = Prog()
    em = p.emit

    x_h = nc.dram_tensor("x", [S, E], F32, kind="ExternalInput")
    wqkv_h = nc.dram_tensor("wqkv", [E, 3 * E], F32, kind="ExternalInput")
    qb_h = nc.dram_tensor("qb", [128, EC], F32, kind="ExternalInput")
    kb_h = nc.dram_tensor("kb", [128, EC], F32, kind="ExternalInput")
    vb_h = nc.dram_tensor("vb", [1, E], F32, kind="ExternalInput")
    wout_h = nc.dram_tensor("wout", [E, E], F32, kind="ExternalInput")
    bout_h = nc.dram_tensor("bout", [1, E], F32, kind="ExternalInput")
    cvec_h = nc.dram_tensor("cvec", [EC, 128], F32, kind="ExternalInput")
    wmod_h = nc.dram_tensor("wmod", [E, 6 * E], F32, kind="ExternalInput")
    wg_h = nc.dram_tensor("wg", [FI, E], F32, kind="ExternalInput")
    wu_h = nc.dram_tensor("wu", [FI, E], F32, kind="ExternalInput")
    wd_h = nc.dram_tensor("wd", [E, FI], F32, kind="ExternalInput")

    x2_h = nc.dram_tensor("x2", [S, E], F32, kind="ExternalOutput")
    hb_h = nc.dram_tensor("hb", [S, E], F32, kind="ExternalOutput")
    m1_h = nc.dram_tensor("m1loc", [1, 1], F32, kind="ExternalOutput")
    gbp_h = nc.dram_tensor("gbp", [1, E], F32, kind="ExternalOutput")
    wgq_h = nc.dram_tensor("wgqT", [E, FI], BF16, kind="ExternalOutput")
    wuq_h = nc.dram_tensor("wuqT", [E, FI], BF16, kind="ExternalOutput")
    wdq_h = nc.dram_tensor("wdqT", [FI, E], BF16, kind="ExternalOutput")
    sg_h = nc.dram_tensor("sg", [128, IC], F32, kind="ExternalOutput")
    su_h = nc.dram_tensor("su", [128, IC], F32, kind="ExternalOutput")
    sd_h = nc.dram_tensor("sd", [128, EC], F32, kind="ExternalOutput")

    with ExitStack() as ctx:
        ab_t = ctx.enter_context(nc.sbuf_tensor("ab", [128, 85400], BF16))
        af_t = ctx.enter_context(nc.sbuf_tensor("af", [128, 10050], F32))
        AB = Arena(ab_t, 85400)
        AFA = Arena(af_t, 10050)

        haT = AB.alloc_n(EC, 128, 512)
        wqkv = AB.alloc_n(EC, 128, 3072)
        KT = AB.alloc_n(EC, 128, S)
        V = AB.alloc_n(NT, 128, 16 * 65)
        QT = AB.alloc_n(2, 128, 512)
        wout = AB.alloc_n(EC, 128, E)
        attnT = AB.alloc_n(EC, 128, 512)
        PT = AB.alloc_n(2, 128, 1024)
        normed = AB.alloc(128, E)
        ovB = AB.alloc(128, 4864)
        wmod_s = [View(ovB[0:128, 512 * i:512 * (i + 1)], ovB.off + 512 * i)
                  for i in range(6)]
        wqi = [View(ovB[0:128, 0:1408], ovB.off),
               View(ovB[0:128, 1408:2816], ovB.off + 1408)]
        wqt = [View(ovB[0:128, 2816:3840], ovB.off + 2816),
               View(ovB[0:128, 3840:4864], ovB.off + 3840)]
        vb_row = AB.alloc(1, E)
        bout_row = AB.alloc(1, E)
        ones_row = AB.alloc(1, 128)
        idt_b = AB.alloc(128, 128)
        siluc = AB.alloc(128, EC)

        xs = AFA.alloc_n(2, 128, E)
        opcb_b, sb_b = xs[0], xs[1]
        ga_b = AFA.alloc(128, E)
        wnat = AFA.alloc_n(3, 128, 1024)
        ov1 = AFA.alloc(128, 3072)
        row_a = View(ov1[0:1, 0:1024], ov1.off)
        row_b = View(ov1[0:1, 1024:2048], ov1.off + 1024)
        row_c = View(ov1[0:1, 2048:3072], ov1.off + 2048)
        hbst = [View(ov1[0:128, 0:1024], ov1.off),
                View(ov1[0:128, 1024:2048], ov1.off + 1024)]
        recips = [View(ov1[0:64, 2048:2560], ov1.off + 2048),
                  View(ov1[0:64, 2560:3072], ov1.off + 2560)]
        onef_row = AFA.alloc(1, 128)
        mod_sa = AFA.alloc(128, EC)
        mod_opca = AFA.alloc(128, EC)
        mod_opcb = AFA.alloc(128, EC)
        mod_sb = AFA.alloc(128, EC)
        qb_cols = AFA.alloc(128, EC)
        kb_cols = AFA.alloc(128, EC)
        st_nm = AFA.alloc(128, NT)
        st_rs = AFA.alloc(128, NT)
        st_t1 = AFA.alloc(128, NT)
        st_t2 = AFA.alloc(128, NT)
        ssum = AFA.alloc(128, NT)
        ssq = AFA.alloc(128, NT)
        s2c = AFA.alloc(128, 6)
        idt_f = AFA.alloc(128, 128)
        iosc = AFA.alloc(128, 128)
        cband = View(iosc[0:EC, 0:128], iosc.off)
        m1run = AFA.alloc(128, 1)
        sc_t = AFA.alloc(128, 4)
        wsc_g = AFA.alloc(128, IC)
        wsc_u = AFA.alloc(128, IC)
        wsc_d = AFA.alloc(128, EC)
        winv_g = AFA.alloc(128, IC)
        winv_u = AFA.alloc(128, IC)
        winv_d = AFA.alloc(128, EC)
        eps_t = AFA.alloc(128, 1)
        magic_t = AFA.alloc(128, 1)
        nmagic_t = AFA.alloc(128, 1)

        psf = [ctx.enter_context(nc.psum_tensor(f"pf{i}", [128, 512], F32))
               for i in range(2)]
        psc = [ctx.enter_context(nc.psum_tensor(f"pc{i}", [128, 1024], F32))
               for i in range(2)]
        psb = [ctx.enter_context(nc.psum_tensor(f"pb{i}", [128, 1024], BF16))
               for i in range(2)]
        PSF = [f"ps{i}" for i in range(2)]
        PSC = [f"psc{i}" for i in range(2)]
        PSB = [f"psb{i}" for i in range(2)]

        ALIAS = {
            "hbst0": ["row_a"], "hbst1": ["row_b"],
            "recip0": ["row_c"], "recip1": ["row_c"],
            "wqi0": ["wmod0", "wmod1"], "wqi1": ["wmod2", "wmod3", "wmod4"],
            "wqt0": ["wmod5"], "wqt1": ["wmod5"],
            "cband": ["iosc"], "iosc": ["cband"],
        }
        _em0 = em

        def em2(stream, fn, reads=(), writes=(), dma=None):
            writes = list(writes)
            for w in list(writes):
                writes += ALIAS.get(w, [])
            return _em0(stream, fn, reads, writes, dma)
        em = em2

        em("gp", lambda g: g.iota(iosc[:], [[1, 128]], channel_multiplier=-1,
                                  allow_small_or_imprecise_dtypes=True),
           writes=["iosc"])
        em("dve", lambda v: v.tensor_single_scalar(idt_b[:], iosc[:], 0.0,
                                                   ALU.is_equal),
           reads=["iosc"], writes=["idt_b"])
        em("dve", lambda v: v.tensor_single_scalar(idt_f[:], iosc[:], 0.0,
                                                   ALU.is_equal),
           reads=["iosc"], writes=["idt_f"])
        em("dve", lambda v: v.memset(ones_row[:], 1.0), writes=["ones_row"])
        em("dve", lambda v: v.memset(onef_row[:], 1.0), writes=["onef_row"])
        em("dve", lambda v: v.memset(m1run[:], 0.0), writes=["m1run"])
        em("dve", lambda v: v.memset(eps_t[:], EPS), writes=["eps_t"])
        em("dve", lambda v: v.memset(magic_t[:], MAGIC), writes=["magic_t"])
        em("dve", lambda v: v.memset(nmagic_t[:], -MAGIC), writes=["nmagic_t"])

        em("qs", lambda s: s.dma_start(out=cband[:], in_=cvec_h[:, :]),
           writes=["cband"], dma="c")
        em("qs", lambda s: s.dma_start(out=qb_cols[:], in_=qb_h[:, :]),
           writes=["qb_cols"], dma="c")
        em("qs", lambda s: s.dma_start(out=kb_cols[:], in_=kb_h[:, :]),
           writes=["kb_cols"], dma="c")
        em("gp", lambda g: g.dma_start(out=vb_row[:], in_=vb_h[:, :]),
           writes=["vb_row"], dma="cg")
        em("gp", lambda g: g.dma_start(out=bout_row[:], in_=bout_h[:, :]),
           writes=["bout_row"], dma="cg")
        em("dve", lambda v: v.tensor_scalar_mul(qb_cols[:], qb_cols[:], 0.125),
           reads=["qb_cols"], writes=["qb_cols"])

        em("act", lambda a: a.activation(cband[:], cband[:], AF.Silu),
           reads=["cband"], writes=["cband"])
        em("pe", lambda t: t.transpose(psc[0][0:128, 0:EC], cband[:],
                                       idt_f[0:EC, 0:EC]),
           reads=["cband", "idt_f"], writes=["psc0"])
        em("dve", lambda v: v.tensor_copy(siluc[:], psc[0][0:128, 0:EC]),
           reads=["psc0"], writes=["siluc"])

        def row_to_cols(row_view, row_key, dst_view, dst_key):
            for c in range(EC):
                em("pe", lambda t, c=c: t.transpose(
                    psc[0][0:128, c:c + 1],
                    row_view[0:1, 128 * c:128 * (c + 1)],
                    idt_f[0:1, 0:1]),
                   reads=[row_key, "idt_f"], writes=["psc0"])
            em("dve", lambda v: v.tensor_copy(dst_view[:], psc[0][0:128, 0:EC]),
               reads=["psc0"], writes=[dst_key])

        def bcast_row(row_view, row_key, dst_view, dst_key):
            for hf in range(2):
                em("pe", lambda t, hf=hf: t.matmul(
                    psc[1][:, 0:512], onef_row[0:1, 0:128],
                    row_view[0:1, 512 * hf:512 * (hf + 1)],
                    start=True, stop=True),
                   reads=[row_key, "onef_row"], writes=["psc1"])
                em("dve", lambda v, hf=hf: v.tensor_copy(
                    dst_view[:, 512 * hf:512 * (hf + 1)], psc[1][:, 0:512]),
                   reads=["psc1"], writes=[dst_key])

        def sign_clamp(src_view, src_key):
            em("dve", lambda v: v.tensor_single_scalar(row_b[:], src_view[:],
                                                       0.0, ALU.is_ge),
               reads=[src_key], writes=["row_b"])
            em("dve", lambda v: v.tensor_scalar(row_b[:], row_b[:], 2.0, -1.0,
                                                ALU.mult, ALU.add),
               reads=["row_b"], writes=["row_b"])
            em("dve", lambda v: v.tensor_mul(row_c[:], src_view[:], row_b[:]),
               reads=[src_key, "row_b"], writes=["row_c"])
            em("dve", lambda v: v.tensor_scalar_max(row_c[:], row_c[:], 0.25),
               reads=["row_c"], writes=["row_c"])
            em("dve", lambda v: v.tensor_mul(row_b[:], row_b[:], row_c[:]),
               reads=["row_b", "row_c"], writes=["row_b"])

        for vv in range(6):
            for half in range(2):
                fg = 2 * vv + half
                for c in range(EC):
                    slot = (8 * fg + c) % 6
                    wk = f"wmod{slot}"
                    qq = "gp"
                    em(qq, lambda g, fg=fg, c=c, slot=slot: g.dma_start(
                        out=wmod_s[slot][:],
                        in_=wmod_h[128 * c:128 * (c + 1),
                                   512 * fg:512 * (fg + 1)]),
                       writes=[wk], dma=f"wm{slot}")
                    em("pe", lambda t, c=c, slot=slot: t.matmul(
                        psf[0][0:1, :], siluc[:, c:c + 1],
                        wmod_s[slot][:], start=(c == 0), stop=(c == EC - 1)),
                       reads=[wk, "siluc"], writes=["ps0"])
                em("dve", lambda v, half=half: v.tensor_copy(
                    row_a[0:1, 512 * half:512 * (half + 1)], psf[0][0:1, :]),
                   reads=["ps0"], writes=["row_a"])
            if vv == 0:
                row_to_cols(row_a, "row_a", mod_sa, "mod_sa")
            elif vv == 1:
                em("dve", lambda v: v.tensor_scalar_add(row_b[:], row_a[:], 1.0),
                   reads=["row_a"], writes=["row_b"])
                row_to_cols(row_b, "row_b", mod_opca, "mod_opca")
            elif vv == 2:
                sign_clamp(row_a, "row_a")
                bcast_row(row_b, "row_b", ga_b, "ga_b")
            elif vv == 3:
                row_to_cols(row_a, "row_a", mod_sb, "mod_sb")
            elif vv == 4:
                em("dve", lambda v: v.tensor_scalar_add(row_b[:], row_a[:], 1.0),
                   reads=["row_a"], writes=["row_b"])
                row_to_cols(row_b, "row_b", mod_opcb, "mod_opcb")
            else:
                sign_clamp(row_a, "row_a")
                em("qs", lambda s: s.dma_start(out=gbp_h[:, :], in_=row_b[:]),
                   reads=["row_b"], dma="sm")

        for tc in range(NT):
            xk = f"xs{tc % 2}"
            em("qs", lambda s, tc=tc: s.dma_start(
                out=xs[tc % 2][:], in_=x_h[128 * tc:128 * (tc + 1), :]),
               writes=[xk], dma=f"x{tc % 2}")
            em("act", lambda a, tc=tc: a.activation(
                normed[:], xs[tc % 2][:], AF.Square,
                accum_out=ssq[:, tc:tc + 1]),
               reads=[xk], writes=["normed", f"ssq{tc}"])
            em("dve", lambda v, tc=tc: v.reduce_sum(
                ssum[:, tc:tc + 1], xs[tc % 2][:], AX.X),
               reads=[xk], writes=[f"ssum{tc}"])
        sskeys = [f"ssq{t}" for t in range(NT)] + [f"ssum{t}" for t in range(NT)]
        em("dve", lambda v: v.tensor_scalar_mul(st_nm[:], ssum[:], -1.0 / E),
           reads=sskeys, writes=["st_nm"])
        em("dve", lambda v: v.tensor_scalar_mul(st_t1[:], ssq[:], 1.0 / E),
           reads=sskeys, writes=["st_t1"])
        em("dve", lambda v: v.tensor_mul(st_t2[:], st_nm[:], st_nm[:]),
           reads=["st_nm"], writes=["st_t2"])
        em("dve", lambda v: v.tensor_sub(st_t1[:], st_t1[:], st_t2[:]),
           reads=["st_t1", "st_t2"], writes=["st_t1"])
        em("act", lambda a: a.activation(st_t2[:], st_t1[:], AF.Sqrt,
                                         bias=eps_t[:, 0:1]),
           reads=["st_t1", "eps_t"], writes=["st_t2"])
        em("dve", lambda v: v.reciprocal(st_rs[:], st_t2[:]),
           reads=["st_t2"], writes=["st_rs"])

        def load_x(tc):
            xk = f"xs{tc % 2}"
            em("qs", lambda s, tc=tc: s.dma_start(
                out=xs[tc % 2][:], in_=x_h[128 * tc:128 * (tc + 1), :]),
               writes=[xk], dma=f"x{tc % 2}")
            return xk

        def make_haT(tc, tc4):
            xk = f"xs{tc % 2}"
            nk = "normed"
            pbk = f"psb{tc4 % 2}"
            em("dve", lambda v, tc=tc: v.tensor_scalar(
                normed[:], xs[tc % 2][:],
                st_nm[:, tc:tc + 1], st_rs[:, tc:tc + 1],
                ALU.add, ALU.mult),
               reads=[xk, "st_nm", "st_rs"], writes=[nk])
            for c in range(EC):
                em("pe", lambda t, tc=tc, tc4=tc4, c=c: t.transpose(
                    psb[tc4 % 2][:, 128 * c:128 * (c + 1)],
                    normed[:][:, 128 * c:128 * (c + 1)], idt_b[:]),
                   reads=[nk, "idt_b"], writes=[pbk])
            for c in range(EC):
                em("dve", lambda v, tc4=tc4, c=c: v.tensor_scalar(
                    haT[c][:, 128 * tc4:128 * (tc4 + 1)],
                    psb[tc4 % 2][:, 128 * c:128 * (c + 1)],
                    mod_opca[:, c:c + 1], mod_sa[:, c:c + 1],
                    ALU.mult, ALU.add),
                   reads=[pbk, "mod_opca", "mod_sa"],
                   writes=[f"haT{c}"])

        haT_keys = [f"haT{c}" for c in range(EC)]

        def seg_widths(rowlen):
            w, off = [], 0
            while off < rowlen:
                w.append(min(1024, rowlen - off))
                off += 1024
            return w

        wq_tasks = ([(wg_h, wgq_h, wsc_g, winv_g, "wsg", i, E)
                     for i in range(IC)]
                    + [(wu_h, wuq_h, wsc_u, winv_u, "wsu", i, E)
                       for i in range(IC)]
                    + [(wd_h, wdq_h, wsc_d, winv_d, "wsd", i, FI)
                       for i in range(EC)])
        _seg_seq = [0]

        def scale_task(task):
            for _t in [task]:
                (w_h, out_h, sc_view, winv_m, sc_key, i, rowlen) = _t
                for hseg, width in enumerate(seg_widths(rowlen)):
                    b = _seg_seq[0] % 3
                    _seg_seq[0] += 1
                    wk = f"wnat{b}"
                    em("gp", lambda g, w_h=w_h, i=i, width=width, b=b,
                       off=1024 * hseg: g.dma_start(
                           out=wnat[b][:][:, 0:width],
                           in_=w_h[128 * i:128 * (i + 1), off:off + width]),
                       writes=[wk], dma=f"wn{b}")
                    em("dve", lambda v, b=b, width=width: v.tensor_reduce(
                        sc_t[:, 2:3], wnat[b][:][:, 0:width], AX.X, ALU.max,
                        apply_absolute_value=True),
                       reads=[wk], writes=["wred"])
                    if hseg == 0:
                        em("dve", lambda v, sc_view=sc_view, i=i: v.tensor_copy(
                            sc_view[:, i:i + 1], sc_t[:, 2:3]),
                           reads=["wred"], writes=[f"{sc_key}{i}"])
                    else:
                        em("dve", lambda v, sc_view=sc_view, i=i: v.tensor_max(
                            sc_view[:, i:i + 1], sc_view[:, i:i + 1],
                            sc_t[:, 2:3]),
                           reads=["wred", f"{sc_key}{i}"],
                           writes=[f"{sc_key}{i}"])
        def scale_finish():
            for sc_view, winv_m, sc_key, n in ((wsc_g, winv_g, "wsg", IC),
                                               (wsc_u, winv_u, "wsu", IC),
                                               (wsc_d, winv_d, "wsd", EC)):
                sks = [f"{sc_key}{i}" for i in range(n)]
                em("dve", lambda v, winv_m=winv_m, sc_view=sc_view:
                   v.reciprocal(winv_m[:], sc_view[:]),
                   reads=sks, writes=[f"inv_{sc_key}"])
                em("dve", lambda v, winv_m=winv_m: v.tensor_scalar_mul(
                    winv_m[:], winv_m[:], 127.0),
                   reads=[f"inv_{sc_key}"], writes=[f"inv_{sc_key}"])
                em("dve", lambda v, sc_view=sc_view: v.tensor_scalar_mul(
                    sc_view[:], sc_view[:], 1.0 / 127.0),
                   reads=sks, writes=[sc_key])
            em("qs", lambda s: s.dma_start(out=sg_h[:, :], in_=wsc_g[:]),
               reads=["wsg"], dma="sm")
            em("qs", lambda s: s.dma_start(out=su_h[:, :], in_=wsc_u[:]),
               reads=["wsu"], dma="sm")
            em("qs", lambda s: s.dma_start(out=sd_h[:, :], in_=wsc_d[:]),
               reads=["wsd"], dma="sm")

        wq_rtasks = []
        for (w_h, out_h, sc_view, winv_m, sc_key, i, rowlen) in wq_tasks:
            for hseg, width in enumerate(seg_widths(rowlen)):
                wq_rtasks.append((w_h, out_h, winv_m, sc_key, i,
                                  1024 * hseg, width))

        def round_load(t):
            (w_h, out_h, winv_m, sc_key, i, off, width) = wq_rtasks[t]
            b = t % 3
            em("gp", lambda g, w_h=w_h, i=i, width=width, b=b, off=off:
               g.dma_start(
                   out=wnat[b][:][:, 0:width],
                   in_=w_h[128 * i:128 * (i + 1), off:off + width]),
               writes=[f"wnat{b}"], dma=f"wn{b}")
            return b

        def round_compute(t, b):
            (w_h, out_h, winv_m, sc_key, i, off, width) = wq_rtasks[t]
            wk = f"wnat{b}"
            qk = f"wqi{t % 2}"
            em("act", lambda a, b=b, width=width, winv_m=winv_m, i=i:
               a.activation(
                   wnat[b][:][:, 0:width], wnat[b][:][:, 0:width],
                   AF.Identity, scale=winv_m[:, i:i + 1],
                   bias=magic_t[:, 0:1]),
               reads=[wk, f"inv_{sc_key}", "magic_t"], writes=[wk])
            em("act", lambda a, b=b, width=width, t=t: a.activation(
                wqi[t % 2][:][:, 0:width], wnat[b][:][:, 0:width],
                AF.Identity, bias=nmagic_t[:, 0:1]),
               reads=[wk, "nmagic_t"], writes=[qk])
            npiece = width // 128
            gp_ = t % 2
            for pp_ in range(npiece):
                em("pe", lambda t_, t=t, pp_=pp_, gp_=gp_: t_.transpose(
                    psb[gp_][:, 128 * pp_:128 * (pp_ + 1)],
                    wqi[t % 2][:][:, 128 * pp_:128 * (pp_ + 1)],
                    idt_b[:]),
                   reads=[qk, "idt_b"], writes=[PSB[gp_]])
            em("dve", lambda v, gp_=gp_, npiece=npiece: v.tensor_copy(
                wqt[gp_][:][:, 0:128 * npiece], psb[gp_][:, 0:128 * npiece]),
               reads=[PSB[gp_]], writes=[f"wqt{gp_}"])
            em("qs", lambda s, out_h=out_h, i=i, off=off, gp_=gp_,
               npiece=npiece: s.dma_start(
                   out=out_h.rearrange("(a p) f -> p a f", p=128)
                   [:, off // 128:off // 128 + npiece,
                    128 * i:128 * (i + 1)],
                   in_=wqt[gp_][:][:, 0:128 * npiece].rearrange(
                       "p (a f) -> p a f", f=128)),
               reads=[f"wqt{gp_}"], dma=f"wt{gp_}")


        wq_bufs = {}
        wq_pos = [0]

        def wq_step(n=1):
            for _ in range(n):
                t = wq_pos[0]
                if t >= len(wq_rtasks):
                    return
                if t + 1 < len(wq_rtasks):
                    wq_bufs[t + 1] = round_load(t + 1)
                round_compute(t, wq_bufs.pop(t))
                wq_pos[0] += 1

        for c in range(EC):
            for gi in range(3):
                em("gp", lambda g, c=c, gi=gi: g.dma_start(
                    out=wqkv[c][:, 1024 * gi:1024 * (gi + 1)].rearrange(
                        "p (h t) -> p h t", h=16),
                    in_=wqkv_h[128 * c:128 * (c + 1), :].rearrange(
                        "p (h t) -> p h t", h=16)[:, :, 64 * gi:64 * (gi + 1)]),
                   writes=[f"wqkv{c}"], dma="wq")
        for tg in range(NG):
            for tc4 in range(4):
                load_x(4 * tg + tc4)
                make_haT(4 * tg + tc4, tc4)
            for j in range(EC):
                pi = j % 2
                pk = PSF[pi]
                for c in range(EC):
                    em("pe", lambda t, j=j, c=c, pi=pi: t.matmul(
                        psf[pi][:, :],
                        wqkv[c][:, 1024 + 128 * j:1024 + 128 * (j + 1)],
                        haT[c][:], start=(c == 0), stop=(c == EC - 1)),
                       reads=[f"wqkv{c}"] + haT_keys, writes=[pk])
                em("dve", lambda v, j=j, tg=tg, pi=pi: v.tensor_single_scalar(
                    KT[j][:, 512 * tg:512 * (tg + 1)], psf[pi][:, :],
                    kb_cols[:, j:j + 1], ALU.add),
                   reads=[pk, "kb_cols"], writes=[f"KT{j}"])
            for tc4 in range(4):
                tc = 4 * tg + tc4
                for half in range(2):
                    pi = (2 * tc4 + half) % 2
                    pk = PSF[pi]
                    for c in range(EC):
                        em("pe", lambda t, tc4=tc4, half=half, c=c, pi=pi:
                           t.matmul(
                               psf[pi][:, :],
                               haT[c][:, 128 * tc4:128 * (tc4 + 1)],
                               wqkv[c][:, 2048 + 512 * half:2048 + 512 * (half + 1)],
                               start=(c == 0), stop=False),
                           reads=[f"wqkv{c}", f"haT{c}"], writes=[pk])
                    em("pe", lambda t, half=half, pi=pi: t.matmul(
                        psf[pi][:, :], ones_row[0:1, 0:128],
                        vb_row[0:1, 512 * half:512 * (half + 1)],
                        start=False, stop=True),
                       reads=["vb_row", "ones_row"], writes=[pk])
                    em("dve", lambda v, tc=tc, half=half, pi=pi: v.tensor_copy(
                        V[tc][:, 520 * half:520 * (half + 1)].rearrange(
                            "p (h c) -> p h c", c=65)[:, :, 0:64],
                        psf[pi][:, :].rearrange("p (h c) -> p h c", c=64)),
                       reads=[pk], writes=[f"V{tc}"])
                em("dve", lambda v, tc=tc: v.memset(
                    V[tc][:, 0:1040].rearrange("p (h c) -> p h c",
                                               c=65)[:, :, 64:65], 1.0),
                   writes=[f"V{tc}"])
            for task in wq_tasks[13 * tg:13 * (tg + 1)]:
                scale_task(task)


        for c in range(EC):
            em("gp", lambda g, c=c: g.dma_start(
                out=wout[c][:], in_=wout_h[128 * c:128 * (c + 1), :]),
               writes=[f"wout{c}"], dma="wo")
        scale_finish()
        wq_bufs[0] = round_load(0)
        wq_pos = [0]
        for qg in range(NG):
            for tc4 in range(4):
                load_x(4 * qg + tc4)
                make_haT(4 * qg + tc4, tc4)
            def qproj(j):
                qb_ = j % 2
                for c in range(EC):
                    em("pe", lambda t, j=j, c=c: t.matmul(
                        psc[0][:, 0:512],
                        wqkv[c][:, 128 * j:128 * (j + 1)],
                        haT[c][:], start=(c == 0), stop=(c == EC - 1)),
                       reads=[f"wqkv{c}"] + haT_keys, writes=["psc0"])
                em("dve", lambda v, j=j, qb_=qb_: v.tensor_scalar(
                    QT[qb_][:], psc[0][:, 0:512], 0.125, qb_cols[:, j:j + 1],
                    ALU.mult, ALU.add),
                   reads=["psc0", "qb_cols"], writes=[f"QT{qb_}"])

            qproj(0)
            for j in range(EC):
                qb_ = j % 2

                def qk(kc, j=j, qb_=qb_):
                    par = kc % 2
                    for hh in range(2):
                        em("pe", lambda t, j=j, kc=kc, hh=hh, par=par,
                           qb_=qb_: t.matmul(
                               psc[par][:, 512 * hh:512 * (hh + 1)],
                               KT[j][64 * hh:64 * (hh + 1),
                                     128 * kc:128 * (kc + 1)],
                               QT[qb_][64 * hh:64 * (hh + 1), :],
                               start=True, stop=True),
                           reads=[f"KT{j}", f"QT{qb_}"], writes=[PSC[par]])

                def expv(kc):
                    par = kc % 2
                    em("act", lambda a, par=par: a.activation(
                        PT[par][:], psc[par][:, :], AF.Exp),
                       reads=[PSC[par]], writes=[f"PT{par}"])

                def pv(kc, j=j):
                    par = kc % 2
                    for hh in range(2):
                        em("pe", lambda t, j=j, kc=kc, hh=hh, par=par: t.matmul(
                            psf[hh][0:65, :],
                            V[kc][:, 65 * (2 * j + hh):65 * (2 * j + hh) + 65],
                            PT[par][:, 512 * hh:512 * (hh + 1)],
                            start=(kc == 0), stop=(kc == NT - 1)),
                           reads=[f"V{kc}", f"PT{par}"], writes=[PSF[hh]])

                qk(0)
                expv(0)
                for kc in range(1, NT):
                    qk(kc)
                    expv(kc)
                    pv(kc - 1)
                pv(NT - 1)
                if j + 1 < EC:
                    qproj(j + 1)
                for hh in range(2):
                    em("act", lambda a, hh=hh: a.activation(
                        recips[hh][0:1, :], psf[hh][64:65, :], AF.Ln),
                       reads=[PSF[hh]], writes=[f"recip{hh}"])
                    em("act", lambda a, hh=hh: a.activation(
                        recips[hh][0:1, :], recips[hh][0:1, :], AF.Exp,
                        scale=-1.0),
                       reads=[f"recip{hh}"], writes=[f"recip{hh}"])
                    em("pe", lambda t, hh=hh: t.matmul(
                        psc[hh][0:64, 0:512], onef_row[0:1, 0:64],
                        recips[hh][0:1, :], start=True, stop=True),
                       reads=[f"recip{hh}", "onef_row"], writes=[PSC[hh]])
                    em("dve", lambda v, hh=hh: v.tensor_copy(
                        recips[hh][0:64, :], psc[hh][0:64, 0:512]),
                       reads=[PSC[hh]], writes=[f"recip{hh}"])
                    em("dve", lambda v, j=j, hh=hh: v.tensor_mul(
                        attnT[j][64 * hh:64 * (hh + 1), :],
                        psf[hh][0:64, :], recips[hh][0:64, :]),
                       reads=[PSF[hh], f"recip{hh}"], writes=[f"attnT{j}"])
                wq_step(2)
            for tp in range(2):
                for half in range(2):
                    for j in range(EC):
                        for u in range(2):
                            tc4 = 2 * tp + u
                            em("pe", lambda t, j=j, tc4=tc4, u=u, half=half:
                               t.matmul(
                                   psc[u][:, 512 * half:512 * (half + 1)],
                                   attnT[j][:, 128 * tc4:128 * (tc4 + 1)],
                                   wout[j][:, 512 * half:512 * (half + 1)],
                                   start=(j == 0), stop=False),
                               reads=[f"attnT{j}", f"wout{j}"], writes=[PSC[u]])
                    for u in range(2):
                        em("pe", lambda t, u=u, half=half: t.matmul(
                            psc[u][:, 512 * half:512 * (half + 1)],
                            ones_row[0:1, 0:128],
                            bout_row[0:1, 512 * half:512 * (half + 1)],
                            start=False, stop=True),
                           reads=["bout_row", "ones_row"], writes=[PSC[u]])
                for u in range(2):
                    tc4 = 2 * tp + u
                    tc = 4 * qg + tc4
                    xk = load_x(tc)
                    for half in range(2):
                        em("dve", lambda v, u=u, half=half: v.tensor_mul(
                            hbst[u][:, 512 * half:512 * (half + 1)],
                            psc[u][:, 512 * half:512 * (half + 1)],
                            ga_b[:, 512 * half:512 * (half + 1)]),
                           reads=[PSC[u], "ga_b"], writes=[f"hbst{u}"])
                    em("dve", lambda v, u=u, tc=tc: v.tensor_add(
                        hbst[u][:], hbst[u][:], xs[tc % 2][:]),
                       reads=[f"hbst{u}", xk], writes=[f"hbst{u}"])
                    em("qs", lambda s, tc=tc, u=u: s.dma_start(
                        out=x2_h[128 * tc:128 * (tc + 1), :], in_=hbst[u][:]),
                       reads=[f"hbst{u}"], dma=f"sx2{u}")
                wq_step(1)



        for colv, ck, dstx, dk in ((mod_opcb, "mod_opcb", opcb_b, "xs0"),
                                   (mod_sb, "mod_sb", sb_b, "xs1")):
            for c in range(EC):
                em("pe", lambda t, colv=colv, c=c: t.transpose(
                    psc[0][0:1, 128 * c:128 * (c + 1)],
                    colv[:, c:c + 1], idt_f[:]),
                   reads=[ck, "idt_f"], writes=["psc0"])
            em("dve", lambda v, dstx=dstx: v.tensor_copy(
                dstx[0:1, 0:E], psc[0][0:1, 0:E]),
               reads=["psc0"], writes=[dk])
            for hf in range(2):
                em("pe", lambda t, dstx=dstx, hf=hf: t.matmul(
                    psc[1][:, 0:512], onef_row[0:1, 0:128],
                    dstx[0:1, 512 * hf:512 * (hf + 1)],
                    start=True, stop=True),
                   reads=[dk, "onef_row"], writes=["psc1"])
                em("dve", lambda v, dstx=dstx, hf=hf: v.tensor_copy(
                    dstx[:, 512 * hf:512 * (hf + 1)], psc[1][:, 0:512]),
                   reads=["psc1"], writes=[dk])
        for tc in range(NT):
            hb_i = tc % 2
            hk = f"hbst{hb_i}"
            em("qs", lambda s, tc=tc, hb_i=hb_i: s.dma_start(
                out=hbst[hb_i][:], in_=x2_h[128 * tc:128 * (tc + 1), :]),
               writes=[hk], dma=f"sx2{hb_i}")
            em("act", lambda a, tc=tc, hb_i=hb_i: a.activation(
                normed[:], hbst[hb_i][:], AF.Square,
                accum_out=s2c[:, 0:1]),
               reads=[hk], writes=["normed", "s2c0"])
            em("dve", lambda v, hb_i=hb_i: v.reduce_sum(
                s2c[:, 1:2], hbst[hb_i][:], AX.X),
               reads=[hk], writes=["s2c1"])
            em("dve", lambda v: v.tensor_scalar_mul(
                s2c[:, 2:3], s2c[:, 1:2], -1.0 / E),
               reads=["s2c1"], writes=["s2c2"])
            em("dve", lambda v: v.tensor_scalar_mul(
                s2c[:, 3:4], s2c[:, 0:1], 1.0 / E),
               reads=["s2c0"], writes=["s2c3"])
            em("dve", lambda v: v.tensor_mul(
                s2c[:, 4:5], s2c[:, 2:3], s2c[:, 2:3]),
               reads=["s2c2"], writes=["s2c4"])
            em("dve", lambda v: v.tensor_sub(
                s2c[:, 3:4], s2c[:, 3:4], s2c[:, 4:5]),
               reads=["s2c3", "s2c4"], writes=["s2c3"])
            em("act", lambda a: a.activation(
                s2c[:, 4:5], s2c[:, 3:4], AF.Sqrt, bias=eps_t[:, 0:1]),
               reads=["s2c3", "eps_t"], writes=["s2c4"])
            em("dve", lambda v: v.reciprocal(s2c[:, 5:6], s2c[:, 4:5]),
               reads=["s2c4"], writes=["s2c5"])
            em("dve", lambda v, hb_i=hb_i: v.tensor_scalar(
                wnat[hb_i][:][:, 0:E], hbst[hb_i][:], s2c[:, 2:3], s2c[:, 5:6],
                ALU.add, ALU.mult),
               reads=[hk, "s2c2", "s2c5"], writes=[f"wnat{hb_i}"])
            em("gp", lambda g, hb_i=hb_i: g.tensor_mul(
                wnat[hb_i][:][:, 0:E], wnat[hb_i][:][:, 0:E], opcb_b[:]),
               reads=[f"wnat{hb_i}", "xs0"], writes=[f"wnat{hb_i}"])
            em("gp", lambda g, hb_i=hb_i: g.tensor_add(
                wnat[hb_i][:][:, 0:E], wnat[hb_i][:][:, 0:E], sb_b[:]),
               reads=[f"wnat{hb_i}", "xs1"], writes=[f"wnat{hb_i}"])
            em("dve", lambda v, hb_i=hb_i: v.tensor_reduce(
                sc_t[:, 0:1], wnat[hb_i][:][:, 0:E], AX.X, ALU.max,
                apply_absolute_value=True),
               reads=[f"wnat{hb_i}"], writes=["sc0"])
            em("dve", lambda v: v.tensor_max(m1run[:], m1run[:],
                                             sc_t[:, 0:1]),
               reads=["m1run", "sc0"], writes=["m1run"])
            em("qs", lambda s, tc=tc, hb_i=hb_i: s.dma_start(
                out=hb_h[128 * tc:128 * (tc + 1), :], in_=wnat[hb_i][:][:, 0:E]),
               reads=[f"wnat{hb_i}"], dma=f"shb{hb_i}")

        em("pe", lambda t: t.transpose(psf[0][0:1, 0:128], m1run[:], idt_f[:]),
           reads=["m1run", "idt_f"], writes=["ps0"])
        em("dve", lambda v: v.tensor_reduce(sc_t[0:1, 1:2],
                                            psf[0][0:1, 0:128], AX.X, ALU.max),
           reads=["ps0"], writes=["sc1"])
        em("qs", lambda s: s.dma_start(out=m1_h[:, :], in_=sc_t[0:1, 1:2]),
           reads=["sc1"], dma="sm")
        while wq_pos[0] < len(wq_rtasks):
            wq_step(1)



        sems = {n: ctx.enter_context(nc.semaphore(n)) for n in p.cum}
        block = ctx.enter_context(nc.Block())
        p.finish(block, sems)
    return nc





def build_l2():
    nc = bass.Bass("TRN2", target_bir_lowering=False)
    p = Prog()

    hb_h = nc.dram_tensor("hb", [S, E], F32, kind="ExternalInput")
    m1v_h = nc.dram_tensor("m1vec", [1, NCORES], F32, kind="ExternalInput")
    clipx_h = nc.dram_tensor("clipx", [1, 1], F32, kind="ExternalInput")
    wgq_h = nc.dram_tensor("wgqT", [E, FI], BF16, kind="ExternalInput")
    wuq_h = nc.dram_tensor("wuqT", [E, FI], BF16, kind="ExternalInput")
    sg_h = nc.dram_tensor("sg", [128, IC], F32, kind="ExternalInput")
    su_h = nc.dram_tensor("su", [128, IC], F32, kind="ExternalInput")

    g_h = nc.dram_tensor("g", [FI, S], F32, kind="ExternalOutput")
    m2_h = nc.dram_tensor("m2loc", [1, 1], F32, kind="ExternalOutput")

    with ExitStack() as ctx:
        ab_t = ctx.enter_context(nc.sbuf_tensor("ab", [128, 64000], BF16))
        af_t = ctx.enter_context(nc.sbuf_tensor("af", [128, 8000], F32))
        AB = Arena(ab_t, 64000)
        AFA = Arena(af_t, 8000)
        em = p.emit

        wgq = AB.alloc_n(EC, 128, FI)
        wuq = AB.alloc_n(EC, 128, FI)
        xqT = AB.alloc_n(EC, 128, S)
        xqs = AB.alloc_n(2, 128, E)
        idt_b = AB.alloc(128, 128)

        hbs = AFA.alloc_n(2, 128, E)
        silu_sb = AFA.alloc_n(2, 128, 512)
        g_sb = AFA.alloc_n(2, 128, 512)
        sg_t = AFA.alloc(128, IC)
        su_t = AFA.alloc(128, IC)
        sgeff = AFA.alloc(128, IC)
        sueff = AFA.alloc(128, IC)
        m1v = AFA.alloc(1, NCORES)
        clipx = AFA.alloc(1, 1)
        scal = AFA.alloc(1, 4)
        inv1b = AFA.alloc(128, 1)
        s1b = AFA.alloc(128, 1)
        magic_t = AFA.alloc(128, 1)
        m2run = AFA.alloc(128, 1)
        sc_t = AFA.alloc(128, 2)
        onef_row = AFA.alloc(1, 128)
        idt_f = AFA.alloc(128, 128)
        iosc = AFA.alloc(128, 128)

        psf = [ctx.enter_context(nc.psum_tensor(f"pf{i}", [128, 512], F32))
               for i in range(6)]
        psb = [ctx.enter_context(nc.psum_tensor(f"pb{i}", [128, 1024], BF16))
               for i in range(2)]
        PSF = [f"ps{i}" for i in range(6)]

        em("gp", lambda g: g.iota(iosc[:], [[1, 128]], channel_multiplier=-1,
                                  allow_small_or_imprecise_dtypes=True),
           writes=["iosc"])
        em("dve", lambda v: v.tensor_single_scalar(idt_b[:], iosc[:], 0.0,
                                                   ALU.is_equal),
           reads=["iosc"], writes=["idt_b"])
        em("dve", lambda v: v.tensor_single_scalar(idt_f[:], iosc[:], 0.0,
                                                   ALU.is_equal),
           reads=["iosc"], writes=["idt_f"])
        em("dve", lambda v: v.memset(onef_row[:], 1.0), writes=["onef_row"])
        em("dve", lambda v: v.memset(magic_t[:], MAGIC), writes=["magic_t"])
        em("dve", lambda v: v.memset(m2run[:], 0.0), writes=["m2run"])

        em("qs", lambda s: s.dma_start(out=m1v[:], in_=m1v_h[:, :]),
           writes=["m1v"], dma="c")
        em("qs", lambda s: s.dma_start(out=clipx[:], in_=clipx_h[:, :]),
           writes=["clipx"], dma="c")
        em("qs", lambda s: s.dma_start(out=sg_t[:], in_=sg_h[:, :]),
           writes=["sg_t"], dma="c")
        em("qs", lambda s: s.dma_start(out=su_t[:], in_=su_h[:, :]),
           writes=["su_t"], dma="c")
        for c in range(EC):
            em("gp", lambda g, c=c: g.dma_start(
                out=wgq[c][:], in_=wgq_h[128 * c:128 * (c + 1), :]),
               writes=[f"wgq{c}"], dma="wg")
            em("gp", lambda g, c=c: g.dma_start(
                out=wuq[c][:], in_=wuq_h[128 * c:128 * (c + 1), :]),
               writes=[f"wuq{c}"], dma="wg")

        em("dve", lambda v: v.tensor_reduce(scal[0:1, 0:1], m1v[:], AX.X,
                                            ALU.max),
           reads=["m1v"], writes=["s_m1"])
        em("act", lambda a: a.activation(scal[0:1, 1:2], clipx[:], AF.Sigmoid),
           reads=["clipx"], writes=["s_sig"])
        em("dve", lambda v: v.tensor_mul(scal[0:1, 0:1], scal[0:1, 0:1],
                                         scal[0:1, 1:2]),
           reads=["s_m1", "s_sig"], writes=["s_s1"])
        em("dve", lambda v: v.reciprocal(scal[0:1, 2:3], scal[0:1, 0:1]),
           reads=["s_s1"], writes=["s_inv"])
        em("dve", lambda v: v.tensor_scalar_mul(scal[0:1, 2:3],
                                                scal[0:1, 2:3], 127.0),
           reads=["s_inv"], writes=["s_inv"])
        em("dve", lambda v: v.tensor_scalar_mul(scal[0:1, 3:4],
                                                scal[0:1, 0:1], 1.0 / 127.0),
           reads=["s_s1"], writes=["s_s1d"])
        for (cidx, dst, dk, rk) in ((2, inv1b, "inv1b", "s_inv"),
                                    (3, s1b, "s1b", "s_s1d")):
            em("pe", lambda t, cidx=cidx: t.matmul(
                psf[0][:, 0:1], onef_row[0:1, 0:128],
                scal[0:1, cidx:cidx + 1], start=True, stop=True),
               reads=[rk, "onef_row"], writes=["ps0"])
            em("dve", lambda v, dst=dst: v.tensor_copy(dst[:], psf[0][:, 0:1]),
               reads=["ps0"], writes=[dk])
        em("dve", lambda v: v.tensor_scalar(sgeff[:], sg_t[:], s1b[:, 0:1],
                                            None, ALU.mult),
           reads=["sg_t", "s1b"], writes=["sgeff"])
        em("dve", lambda v: v.tensor_scalar(sueff[:], su_t[:], s1b[:, 0:1],
                                            None, ALU.mult),
           reads=["su_t", "s1b"], writes=["sueff"])

        for tc in range(NT):
            hk = f"hbs{tc % 2}"
            em("qs", lambda s, tc=tc: s.dma_start(
                out=hbs[tc % 2][:], in_=hb_h[128 * tc:128 * (tc + 1), :]),
               writes=[hk], dma=f"x{tc % 2}")
            em("act", lambda a, tc=tc: a.activation(
                hbs[tc % 2][:], hbs[tc % 2][:], AF.Identity,
                scale=inv1b[:, 0:1], bias=magic_t[:, 0:1]),
               reads=[hk, "inv1b", "magic_t"], writes=[hk])
            em("dve", lambda v, tc=tc: v.tensor_scalar(
                hbs[tc % 2][:], hbs[tc % 2][:], MAGIC, 127.0,
                ALU.subtract, ALU.min),
               reads=[hk], writes=[hk])
            em("dve", lambda v, tc=tc: v.tensor_scalar_max(
                xqs[tc % 2][:], hbs[tc % 2][:], -128.0),
               reads=[hk], writes=[f"xqs{tc % 2}"])
            for c in range(EC):
                em("pe", lambda t, tc=tc, c=c: t.transpose(
                    psb[tc % 2][:, 128 * c:128 * (c + 1)],
                    xqs[tc % 2][:][:, 128 * c:128 * (c + 1)], idt_b[:]),
                   reads=[f"xqs{tc % 2}", "idt_b"], writes=[f"psb{tc % 2}"])
            for c in range(EC):
                em("dve", lambda v, tc=tc, c=c: v.tensor_copy(
                    xqT[c][:, 128 * tc:128 * (tc + 1)],
                    psb[tc % 2][:, 128 * c:128 * (c + 1)]),
                   reads=[f"psb{tc % 2}"], writes=[f"xqT{c}"])

        xqT_keys = [f"xqT{c}" for c in range(EC)]
        for i in range(IC):
            for tg in range(NG):
                pg = tg % 2
                pu = 2 + tg % 2
                for c in range(EC):
                    em("pe", lambda t, i=i, tg=tg, c=c, pg=pg: t.matmul(
                        psf[pg][:, :], wgq[c][:, 128 * i:128 * (i + 1)],
                        xqT[c][:, 512 * tg:512 * (tg + 1)],
                        start=(c == 0), stop=(c == EC - 1)),
                       reads=[f"wgq{c}"] + xqT_keys, writes=[PSF[pg]])
                for c in range(EC):
                    em("pe", lambda t, i=i, tg=tg, c=c, pu=pu: t.matmul(
                        psf[pu][:, :], wuq[c][:, 128 * i:128 * (i + 1)],
                        xqT[c][:, 512 * tg:512 * (tg + 1)],
                        start=(c == 0), stop=(c == EC - 1)),
                       reads=[f"wuq{c}"] + xqT_keys, writes=[PSF[pu]])
                em("act", lambda a, i=i, tg=tg, pg=pg: a.activation(
                    silu_sb[tg % 2][:], psf[pg][:, :], AF.Silu,
                    scale=sgeff[:, i:i + 1]),
                   reads=[PSF[pg], "sgeff"], writes=[f"silu{tg % 2}"])
                em("dve", lambda v, tg=tg, pu=pu: v.tensor_mul(
                    g_sb[tg % 2][:], silu_sb[tg % 2][:], psf[pu][:, :]),
                   reads=[f"silu{tg % 2}", PSF[pu]], writes=[f"g{tg % 2}"])
                em("dve", lambda v, tg=tg: v.tensor_reduce(
                    sc_t[:, 0:1], g_sb[tg % 2][:], AX.X, ALU.max,
                    apply_absolute_value=True),
                   reads=[f"g{tg % 2}"], writes=["sc0"])
                em("dve", lambda v, i=i: v.tensor_scalar(
                    sc_t[:, 1:2], sc_t[:, 0:1], sueff[:, i:i + 1], None,
                    ALU.mult),
                   reads=["sc0", "sueff"], writes=["sc1"])
                em("dve", lambda v: v.tensor_max(m2run[:], m2run[:],
                                                 sc_t[:, 1:2]),
                   reads=["m2run", "sc1"], writes=["m2run"])
                em("qs", lambda s, i=i, tg=tg: s.dma_start(
                    out=g_h[128 * i:128 * (i + 1), 512 * tg:512 * (tg + 1)],
                    in_=g_sb[tg % 2][:]),
                   reads=[f"g{tg % 2}"], dma=f"sg{tg % 2}")

        em("pe", lambda t: t.transpose(psf[0][0:1, 0:128], m2run[:], idt_f[:]),
           reads=["m2run", "idt_f"], writes=["ps0"])
        em("dve", lambda v: v.tensor_reduce(sc_t[0:1, 0:1],
                                            psf[0][0:1, 0:128], AX.X, ALU.max),
           reads=["ps0"], writes=["sc0f"])
        em("qs", lambda s: s.dma_start(out=m2_h[:, :], in_=sc_t[0:1, 0:1]),
           reads=["sc0f"], dma="sm")

        sems = {n: ctx.enter_context(nc.semaphore(n)) for n in p.cum}
        block = ctx.enter_context(nc.Block())
        p.finish(block, sems)
    return nc



def build_l3():
    nc = bass.Bass("TRN2", target_bir_lowering=False)
    p = Prog()

    g_h = nc.dram_tensor("g", [FI, S], F32, kind="ExternalInput")
    m1v_h = nc.dram_tensor("m1vec", [1, NCORES], F32, kind="ExternalInput")
    m2v_h = nc.dram_tensor("m2vec", [1, NCORES], F32, kind="ExternalInput")
    clipx_h = nc.dram_tensor("clipx", [1, 1], F32, kind="ExternalInput")
    clipd_h = nc.dram_tensor("clipd", [1, 1], F32, kind="ExternalInput")
    su_h = nc.dram_tensor("su", [128, IC], F32, kind="ExternalInput")
    wdq_h = nc.dram_tensor("wdqT", [FI, E], BF16, kind="ExternalInput")
    sd_h = nc.dram_tensor("sd", [128, EC], F32, kind="ExternalInput")
    x2_h = nc.dram_tensor("x2", [S, E], F32, kind="ExternalInput")
    gbp_h = nc.dram_tensor("gbp", [1, E], F32, kind="ExternalInput")

    out_h = nc.dram_tensor("out", [S, E], F32, kind="ExternalOutput")

    with ExitStack() as ctx:
        ab_t = ctx.enter_context(nc.sbuf_tensor("ab", [128, 48000], BF16))
        af_t = ctx.enter_context(nc.sbuf_tensor("af", [128, 9600], F32))
        AB = Arena(ab_t, 48000)
        AFA = Arena(af_t, 9600)
        em = p.emit

        wdq = AB.alloc_n(IC, 128, E)
        hqT = [AB.alloc_n(IC, 128, 512), AB.alloc_n(IC, 128, 512)]
        idt_b = AB.alloc(128, 128)

        gst = AFA.alloc_n(3, 128, 512)
        x2s = AFA.alloc_n(2, 128, E)
        outs = AFA.alloc_n(2, 128, E)
        F_b = AFA.alloc(128, E)
        F_row = AFA.alloc(1, E)
        sd_t = AFA.alloc(128, EC)
        su_t = AFA.alloc(128, IC)
        qf = AFA.alloc(128, IC)
        m1v = AFA.alloc(1, NCORES)
        m2v = AFA.alloc(1, NCORES)
        clipx = AFA.alloc(1, 1)
        clipd = AFA.alloc(1, 1)
        gbp = AFA.alloc(1, E)
        scal = AFA.alloc(1, 6)
        inv2b = AFA.alloc(128, 1)
        magic_t = AFA.alloc(128, 1)
        onef_row = AFA.alloc(1, 128)
        idt_f = AFA.alloc(128, 128)
        iosc = AFA.alloc(128, 128)

        psf = [ctx.enter_context(nc.psum_tensor(f"pf{i}", [128, 512], F32))
               for i in range(4)]
        PSF = [f"ps{i}" for i in range(4)]

        em("gp", lambda g: g.iota(iosc[:], [[1, 128]], channel_multiplier=-1,
                                  allow_small_or_imprecise_dtypes=True),
           writes=["iosc"])
        em("dve", lambda v: v.tensor_single_scalar(idt_b[:], iosc[:], 0.0,
                                                   ALU.is_equal),
           reads=["iosc"], writes=["idt_b"])
        em("dve", lambda v: v.tensor_single_scalar(idt_f[:], iosc[:], 0.0,
                                                   ALU.is_equal),
           reads=["iosc"], writes=["idt_f"])
        em("dve", lambda v: v.memset(onef_row[:], 1.0), writes=["onef_row"])
        em("dve", lambda v: v.memset(magic_t[:], MAGIC), writes=["magic_t"])

        for nm, dst, hh in (("m1v", m1v, m1v_h), ("m2v", m2v, m2v_h),
                            ("clipx", clipx, clipx_h), ("clipd", clipd, clipd_h),
                            ("su_t", su_t, su_h), ("sd_t", sd_t, sd_h),
                            ("gbp", gbp, gbp_h)):
            em("qs", lambda s, dst=dst, hh=hh: s.dma_start(out=dst[:],
                                                           in_=hh[:, :]),
               writes=[nm], dma="c")
        for i in range(IC):
            em("gp", lambda g, i=i: g.dma_start(
                out=wdq[i][:], in_=wdq_h[128 * i:128 * (i + 1), :]),
               writes=[f"wdq{i}"], dma="wd")

        em("dve", lambda v: v.tensor_reduce(scal[0:1, 0:1], m1v[:], AX.X,
                                            ALU.max),
           reads=["m1v"], writes=["s_m1"])
        em("act", lambda a: a.activation(scal[0:1, 1:2], clipx[:], AF.Sigmoid),
           reads=["clipx"], writes=["s_sx"])
        em("dve", lambda v: v.tensor_mul(scal[0:1, 0:1], scal[0:1, 0:1],
                                         scal[0:1, 1:2]),
           reads=["s_m1", "s_sx"], writes=["s_s1"])
        em("dve", lambda v: v.tensor_reduce(scal[0:1, 2:3], m2v[:], AX.X,
                                            ALU.max),
           reads=["m2v"], writes=["s_m2"])
        em("act", lambda a: a.activation(scal[0:1, 3:4], clipd[:], AF.Sigmoid),
           reads=["clipd"], writes=["s_sd"])
        em("dve", lambda v: v.tensor_mul(scal[0:1, 2:3], scal[0:1, 2:3],
                                         scal[0:1, 3:4]),
           reads=["s_m2", "s_sd"], writes=["s_s2m"])
        em("dve", lambda v: v.reciprocal(scal[0:1, 4:5], scal[0:1, 2:3]),
           reads=["s_s2m"], writes=["s_i2"])
        em("dve", lambda v: v.tensor_scalar_mul(scal[0:1, 5:6],
                                                scal[0:1, 2:3], 1.0 / 127.0),
           reads=["s_s2m"], writes=["s_s2"])
        em("dve", lambda v: v.tensor_mul(scal[0:1, 0:1], scal[0:1, 0:1],
                                         scal[0:1, 4:5]),
           reads=["s_s1", "s_i2"], writes=["s_q"])
        em("pe", lambda t: t.matmul(psf[0][:, 0:1], onef_row[0:1, 0:128],
                                    scal[0:1, 0:1], start=True, stop=True),
           reads=["s_q", "onef_row"], writes=["ps0"])
        em("dve", lambda v: v.tensor_copy(inv2b[:], psf[0][:, 0:1]),
           reads=["ps0"], writes=["inv2b"])
        em("dve", lambda v: v.tensor_scalar(qf[:], su_t[:], inv2b[:, 0:1],
                                            None, ALU.mult),
           reads=["su_t", "inv2b"], writes=["qf"])
        for c in range(EC):
            em("pe", lambda t, c=c: t.transpose(
                psf[0][0:1, 128 * c:128 * (c + 1) - 384] if False else
                psf[0][0:1, 0:128],
                sd_t[:, c:c + 1], idt_f[:]),
               reads=["sd_t", "idt_f"], writes=["ps0"])
            em("dve", lambda v, c=c: v.tensor_copy(
                F_row[0:1, 128 * c:128 * (c + 1)], psf[0][0:1, 0:128]),
               reads=["ps0"], writes=["F_row"])
        em("dve", lambda v: v.tensor_mul(F_row[:], F_row[:], gbp[:]),
           reads=["F_row", "gbp"], writes=["F_row"])
        em("dve", lambda v: v.tensor_scalar(F_row[:], F_row[:],
                                            scal[0:1, 5:6], None, ALU.mult),
           reads=["F_row", "s_s2"], writes=["F_row"])
        for hf in range(2):
            em("pe", lambda t, hf=hf: t.matmul(
                psf[1][:, :], onef_row[0:1, 0:128],
                F_row[0:1, 512 * hf:512 * (hf + 1)], start=True, stop=True),
               reads=["F_row", "onef_row"], writes=["ps1"])
            em("dve", lambda v, hf=hf: v.tensor_copy(
                F_b[:, 512 * hf:512 * (hf + 1)], psf[1][:, :]),
               reads=["ps1"], writes=["F_b"])

        def quant_tg(tg):
            par = tg % 2
            for i in range(IC):
                gk = f"gst{(22 * tg + i) % 3}"
                gb = (22 * tg + i) % 3
                em("gp", lambda g, i=i, tg=tg, gb=gb: g.dma_start(
                    out=gst[gb][:],
                    in_=g_h[128 * i:128 * (i + 1), 512 * tg:512 * (tg + 1)]),
                   writes=[gk], dma=f"g{gb}")
                em("act", lambda a, i=i, gb=gb: a.activation(
                    gst[gb][:], gst[gb][:], AF.Identity,
                    scale=qf[:, i:i + 1], bias=magic_t[:, 0:1]),
                   reads=[gk, "qf", "magic_t"], writes=[gk])
                em("dve", lambda v, i=i, gb=gb: v.tensor_scalar(
                    gst[gb][:], gst[gb][:], MAGIC, 127.0,
                    ALU.subtract, ALU.min),
                   reads=[gk], writes=[gk])
                em("dve", lambda v, i=i, gb=gb, par=par: v.tensor_scalar_max(
                    hqT[par][i][:], gst[gb][:], -128.0),
                   reads=[gk], writes=[f"hqT{par}_{i}"])

        def mm_tg(tg):
            par = tg % 2
            for tc4 in range(4):
                tc = 4 * tg + tc4
                xk = f"x2s{tc % 2}"
                em("qs", lambda s, tc=tc: s.dma_start(
                    out=x2s[tc % 2][:], in_=x2_h[128 * tc:128 * (tc + 1), :]),
                   writes=[xk], dma=f"x{tc % 2}")
                for hg in range(2):
                    pi = hg
                    pk = PSF[pi]
                    for i in range(IC):
                        em("pe", lambda t, i=i, tc4=tc4, hg=hg, pi=pi, par=par:
                           t.matmul(
                               psf[pi][:, :],
                               hqT[par][i][:, 128 * tc4:128 * (tc4 + 1)],
                               wdq[i][:, 512 * hg:512 * (hg + 1)],
                               start=(i == 0), stop=(i == IC - 1)),
                           reads=[f"hqT{par}_{i}", f"wdq{i}"], writes=[pk])
                    em("dve", lambda v, tc=tc, hg=hg, pi=pi: v.tensor_mul(
                        outs[tc % 2][:][:, 512 * hg:512 * (hg + 1)],
                        psf[pi][:, :], F_b[:, 512 * hg:512 * (hg + 1)]),
                       reads=[pk, "F_b"], writes=[f"outs{tc % 2}"])
                    em("dve", lambda v, tc=tc, hg=hg: v.tensor_add(
                        outs[tc % 2][:][:, 512 * hg:512 * (hg + 1)],
                        outs[tc % 2][:][:, 512 * hg:512 * (hg + 1)],
                        x2s[tc % 2][:][:, 512 * hg:512 * (hg + 1)]),
                       reads=[f"outs{tc % 2}", xk], writes=[f"outs{tc % 2}"])
                em("qs", lambda s, tc=tc: s.dma_start(
                    out=out_h[128 * tc:128 * (tc + 1), :], in_=outs[tc % 2][:]),
                   reads=[f"outs{tc % 2}"], dma=f"so{tc % 2}")

        quant_tg(0)
        for tg in range(NG):
            if tg + 1 < NG:
                quant_tg(tg + 1)
            mm_tg(tg)
        sems = {n: ctx.enter_context(nc.semaphore(n)) for n in p.cum}
        block = ctx.enter_context(nc.Block())
        p.finish(block, sems)
    return nc



_CACHE = {}


def _programs():
    if "l1" not in _CACHE:
        _CACHE["l1"] = build_l1()
        _CACHE["l2"] = build_l2()
        _CACHE["l3"] = build_l3()
    return _CACHE["l1"], _CACHE["l2"], _CACHE["l3"]


def kernel(L0_attn_x, L0_attn_Wqkv, L0_attn_bqkv, L0_attn_Wout, L0_attn_bout,
           L0_ffn_clip_x, L0_ffn_clip_down, L0_ffn_w_gate, L0_ffn_w_up,
           L0_ffn_w_down, L0_c, L0_wmod):
    f32 = lambda a: np.ascontiguousarray(np.asarray(a, dtype=np.float32))
    x = f32(L0_attn_x)
    wqkv = f32(L0_attn_Wqkv)
    bqkv = f32(L0_attn_bqkv)
    wout = f32(L0_attn_Wout)
    bout = f32(L0_attn_bout).reshape(1, E)
    cmat = f32(L0_c)
    wmod = f32(L0_wmod)
    wg = f32(L0_ffn_w_gate)
    wu = f32(L0_ffn_w_up)
    wd = f32(L0_ffn_w_down)
    clipx = f32(L0_ffn_clip_x).reshape(1, 1)
    clipd = f32(L0_ffn_clip_down).reshape(1, 1)

    jj = np.arange(EC)
    pp = np.arange(128)
    qb = np.ascontiguousarray(
        bqkv[384 * jj[None, :] + 192 * (pp[:, None] // 64) + (pp[:, None] % 64)])
    kb = np.ascontiguousarray(
        bqkv[384 * jj[None, :] + 64 + 192 * (pp[:, None] // 64)
             + (pp[:, None] % 64)])
    hh = np.arange(16)
    dd = np.arange(64)
    vb = np.ascontiguousarray(
        bqkv[192 * hh[:, None] + 128 + dd[None, :]].reshape(1, E))

    l1, l2, l3 = _programs()

    in1 = [{
        "x": np.ascontiguousarray(x[b]),
        "wqkv": wqkv, "qb": qb, "kb": kb, "vb": vb,
        "wout": wout, "bout": bout,
        "cvec": np.ascontiguousarray(cmat[b].reshape(EC, 128)),
        "wmod": wmod, "wg": wg, "wu": wu, "wd": wd,
    } for b in range(NCORES)]
    r1 = run_bass_kernel_spmd(l1, in1, core_ids=list(range(NCORES))).results

    m1vec = np.ascontiguousarray(
        np.stack([r1[b]["m1loc"][0, 0] for b in range(NCORES)])[None, :])
    in2 = [{
        "hb": r1[b]["hb"], "m1vec": m1vec, "clipx": clipx,
        "wgqT": r1[b]["wgqT"], "wuqT": r1[b]["wuqT"],
        "sg": r1[b]["sg"], "su": r1[b]["su"],
    } for b in range(NCORES)]
    r2 = run_bass_kernel_spmd(l2, in2, core_ids=list(range(NCORES))).results

    m2vec = np.ascontiguousarray(
        np.stack([r2[b]["m2loc"][0, 0] for b in range(NCORES)])[None, :])
    in3 = [{
        "g": r2[b]["g"], "m1vec": m1vec, "m2vec": m2vec,
        "clipx": clipx, "clipd": clipd, "su": r1[b]["su"],
        "wdqT": r1[b]["wdqT"], "sd": r1[b]["sd"],
        "x2": r1[b]["x2"], "gbp": r1[b]["gbp"],
    } for b in range(NCORES)]
    r3 = run_bass_kernel_spmd(l3, in3, core_ids=list(range(NCORES))).results

    out = np.stack([r3[b]["out"] for b in range(NCORES)], axis=0)
    return out.astype(np.float32)

